# version 07
# baseline (speedup 1.0000x reference)
import math
import numpy as np
import ml_dtypes
import concourse.bass as bass
import concourse.mybir as mybir
from concourse.bass_utils import run_bass_kernel_spmd

F32 = mybir.dt.float32
BF16 = mybir.dt.bfloat16
U8 = mybir.dt.uint8
ALU = mybir.AluOpType
ACT = mybir.ActivationFunctionType
AXX = mybir.AxisListType.X

B, T, D = 4, 2048, 2048
H, DH = 16, 128
DFF = 8192
NT = 16
NOWN = 8
TOWN = NOWN * 128
ND = D // 128
NO = DFF // 128

SM_SCALE = 1.0 / math.sqrt(128.0)
EPS_LN = 1e-5
EPS_BL = 1e-6
QB = 128.0
CLIP_HI = 128.0 - 1e-6
CLIP_LO = -128.0 + 1e-6
NW = float(DFF * D)

ARENA = 212832


DEBUG = False


def build_program():
    nc = bass.Bass("TRN2", target_bir_lowering=False)
    ikind = "ExternalOutput" if DEBUG else "Internal"

    xkv = nc.dram_tensor("xkv", [T, D], F32, kind="ExternalInput")
    xq = nc.dram_tensor("xq", [TOWN, D], F32, kind="ExternalInput")
    wq = nc.dram_tensor("wq", [D, D], F32, kind="ExternalInput")
    wk = nc.dram_tensor("wk", [D, D], F32, kind="ExternalInput")
    wv = nc.dram_tensor("wv", [D, D], F32, kind="ExternalInput")
    wo = nc.dram_tensor("wo", [D, D], F32, kind="ExternalInput")
    bq = nc.dram_tensor("bq", [128, ND], F32, kind="ExternalInput")
    bk = nc.dram_tensor("bk", [128, ND], F32, kind="ExternalInput")
    bv = nc.dram_tensor("bv", [1, D], F32, kind="ExternalInput")
    bo = nc.dram_tensor("bo", [1, D], F32, kind="ExternalInput")
    wfcT = nc.dram_tensor("wfcT", [D, DFF], F32, kind="ExternalInput")
    wprojT = nc.dram_tensor("wprojT", [DFF, D], F32, kind="ExternalInput")
    maska = nc.dram_tensor("maska", [128, 128], F32, kind="ExternalInput")
    maskb = nc.dram_tensor("maskb", [128, 128], F32, kind="ExternalInput")
    ident_in = nc.dram_tensor("ident", [128, 128], BF16, kind="ExternalInput")
    out = nc.dram_tensor("out", [TOWN, D], F32, kind="ExternalOutput")

    tfc_d = nc.dram_tensor("tfc_d", [D, DFF], BF16, kind=ikind)
    tpj_d = nc.dram_tensor("tpj_d", [DFF, D], BF16, kind=ikind)
    v_d = nc.dram_tensor("v_d", [H * T, DH], BF16, kind=ikind)
    xnew_d = nc.dram_tensor("xnew_d", [TOWN, D], F32, kind=ikind)
    xhat_d = nc.dram_tensor("xhat_d", [TOWN, D], BF16, kind=ikind)
    hhat_d = nc.dram_tensor("hhat_d", [TOWN, DFF], BF16, kind=ikind)
    gpart_d = nc.dram_tensor("gpart_d", [1, 1024], F32, kind=ikind)
    if DEBUG:
        dbg_x1 = nc.dram_tensor("dbg_x1", [128, ND * 1024], BF16, kind="ExternalOutput")
        dbg_k = nc.dram_tensor("dbg_k", [128, H * T], BF16, kind="ExternalOutput")
        dbg_q = nc.dram_tensor("dbg_q", [128, H * TOWN], BF16, kind="ExternalOutput")
        dbg_st = nc.dram_tensor("dbg_st", [128, 64], F32, kind="ExternalOutput")
        dbg_attn = nc.dram_tensor("dbg_attn", [128, H * TOWN], BF16, kind="ExternalOutput")

    mgr = []
    arena_cm = nc.sbuf_tensor("arena", [128, ARENA], U8)
    arena = arena_cm.__enter__()
    mgr.append(arena_cm)
    arena_base = nc.lookup_mloc(arena).addr

    def place(name, shape, dtype, off):
        h = nc.virtual_sbuf_tensor(name, shape, dtype)
        nc._tensor(h.name, h.shape, h.dtype, type="SB")
        mloc = nc.lookup_mloc(h)
        mloc.allocated = True
        mloc.addr = arena_base + off
        return h

    ident = place("identb", [128, 128], BF16, 0)
    mska = place("mska", [128, 128], F32, 256)
    mskb = place("mskb", [128, 128], F32, 768)
    bq_sb = place("bq_sb", [128, ND], F32, 1280)
    bk_sb = place("bk_sb", [128, ND], F32, 1344)
    ones_f = place("ones_f", [1, 128], F32, 1408)
    ones_b = place("ones_b", [1, 128], BF16, 1920)
    ones_c = place("ones_c", [128, 1], F32, 2176)
    st = place("st", [128, 64], F32, 2208)
    gval = place("gval", [1, 8], F32, 2464)
    thr = place("thr", [128, 8], F32, 2496)
    sfc = place("sfc", [128, 8], F32, 2528)
    spj = place("spj", [128, 8], F32, 2560)
    expb = place("expb", [128, 2], F32, 2592)
    lsum = place("lsum", [128, 4], F32, 2624)
    gacc = place("gacc", [128, 256], F32, 2656)
    junk = place("junk", [128, 512], F32, 3712)
    wch_a = place("wch_a", [128, 512], BF16, 5760)
    wch_b = place("wch_b", [128, 512], BF16, 6784)
    xq_c = place("xq_c", [128, 512], F32, 7808)
    out_c = place("out_c", [128, 512], F32, 9856)
    uq = place("uq", [128, 1024], F32, 11904)
    xhat_t = place("xhat_t", [128, 2048], BF16, 16000)
    x1t = place("x1t", [128, 2048], BF16, 20096)
    bv_sb = place("bv_sb", [1, 2048], BF16, 24192)
    bo_sb = place("bo_sb", [1, 2048], BF16, 28288)
    xt = place("xt", [128, 2048], F32, 32384)
    p_sb = place("p_sb", [128, 2048], BF16, 40576)
    pt_sb = place("pt_sb", [128, 1024], BF16, 44672)
    vbc = place("vbc", [128, 512], BF16, 46720)
    P_END = 47744
    A1 = P_END
    A2 = A1 + 65536
    A3 = A2 + 32768
    TRS = A3 + 32768
    k_sb = place("k_sb", [128, H * T], BF16, A1)
    xnew_sb = place("xnew_sb", [128, NOWN * D], F32, A1)
    h_lo = place("h_lo", [128, 4 * DFF], BF16, A1)
    x1kv = place("x1kv", [128, ND * 1024], BF16, A2)
    q_sb = place("q_sb", [128, H * TOWN], BF16, A2)
    wsl_kv = place("wsl_kv", [128, ND * 512], BF16, A3)
    x1q = place("x1q", [128, ND * 512], BF16, A3)
    wsl_q = place("wsl_q", [128, ND * 512], BF16, A3 + 16384)
    attn_sb = place("attn_sb", [128, H * TOWN], BF16, A3)
    h_hi = place("h_hi", [128, 4 * DFF], BF16, A3)
    g_a = place("g_a", [128, 1024], F32, TRS)
    g_b = place("g_b", [128, 1024], F32, TRS + 4096)
    w2 = place("w2", [128, 2048], F32, TRS)
    tin_a = place("tin_a", [128, 1024], F32, TRS + 8192)
    tin_b = place("tin_b", [128, 1024], F32, TRS + 12288)
    tm1 = place("tm1", [128, 1024], BF16, TRS + 16384)
    tm0 = place("tm0", [128, 1024], BF16, TRS + 18432)
    to1 = place("to1", [128, 1024], BF16, TRS + 20480)
    to2 = place("to2", [128, 1024], BF16, TRS + 22528)
    vs_a = place("vs_a", [128, 2048], BF16, TRS + 24576)
    vs_b = place("vs_b", [128, 2048], BF16, TRS + 28672)
    assert TRS + 32768 <= ARENA
    assert A3 + 65536 <= ARENA

    def h_ap(t, c0, w):
        base = h_lo if t < 4 else h_hi
        tt = t % 4
        return base[:, tt * DFF + c0: tt * DFF + c0 + w]

    def hfm_ap(o, c0, w):
        base = h_lo if o < 32 else h_hi
        oo = o % 32
        return base[:, oo * 1024 + c0: oo * 1024 + c0 + w]

    def psum(name, shape, dt):
        cm = nc.psum_tensor(name, shape, dt)
        h = cm.__enter__()
        mgr.append(cm)
        return h

    s_ps = psum("s_ps", [128, 2048], F32)
    pt_ps = psum("pt_ps", [128, 1024], F32)
    av_ps = psum("av_ps", [128, 512], F32)
    sm_ps = psum("sm_ps", [128, 512], F32)

    def bank_ap(b, w=512):
        t = [s_ps, s_ps, s_ps, s_ps, pt_ps, pt_ps, av_ps, sm_ps][b]
        off = [0, 512, 1024, 1536, 0, 512, 0, 0][b]
        return t[:, off: off + w]

    sem_names = ["sq", "gq", "cq", "tv", "vv", "sv", "gv"]
    SEM = {}
    for s in sem_names:
        cm = nc.semaphore(s)
        SEM[s] = cm.__enter__()
        mgr.append(cm)
    C = {s: 0 for s in sem_names}
    OPS = {"sync": [], "gpsimd": [], "tensor": [], "vector": [], "scalar": []}
    bank_ev = {i: [] for i in range(8)}

    def E(eng, fn, wait=(), inc=None, n=1):
        OPS[eng].append((list(wait), fn, inc, n))
        if inc is not None:
            C[inc] += n
        return C[inc] if inc else None

    def VS():
        return [("vv", C["vv"])]

    def SS():
        return [("sv", C["sv"])]

    def EV(fn, wait=(), inc="vv"):
        return E("vector", fn, wait=list(wait) + [("vv", C["vv"])], inc=inc)

    def ES(fn, wait=(), inc="sv"):
        return E("scalar", fn, wait=list(wait) + [("sv", C["sv"])], inc=inc)

    def dma(eng, sem, dst, src, wait=()):
        E(eng, lambda e, d=dst, s=src: e.dma_start(out=d, in_=s), wait=wait, inc=sem, n=16)

    def dmat(dst, src, wait=()):
        E("sync", lambda e, d=dst, s=src: e.dma_start(out=d, in_=s, transpose=True),
          wait=wait, inc="sq", n=16)

    def bwait(*banks):
        w = []
        for b in banks:
            for p in bank_ev[b]:
                if p not in w:
                    w.append(p)
        return w

    def bset(*banks, ev):
        for b in banks:
            bank_ev[b] = list(ev)

    dma("sync", "sq", ident[:, :], ident_in[:, :])
    dma("sync", "sq", mska[:, :], maska[:, :])
    dma("sync", "sq", mskb[:, :], maskb[:, :])
    dma("sync", "sq", bq_sb[:, :], bq[:, :])
    dma("sync", "sq", bk_sb[:, :], bk[:, :])
    dma("gpsimd", "gq", bv_sb[:, :], bv[:, :])
    dma("gpsimd", "gq", bo_sb[:, :], bo[:, :])
    consts_sq = C["sq"]
    consts_gq = C["gq"]
    E("gpsimd", lambda e: e.memset(ones_f[:, :], 1.0))
    E("gpsimd", lambda e: e.memset(ones_b[:, :], 1.0))
    E("gpsimd", lambda e: e.memset(ones_c[:, :], 1.0), inc="gv")
    consts_gv = C["gv"]

    def gsrc(i):
        if i < 128:
            s = i
            return wfcT[(s // 8) * 128:(s // 8 + 1) * 128, (s % 8) * 1024:(s % 8 + 1) * 1024]
        s = i - 128
        return wprojT[(s // 2) * 128:(s // 2 + 1) * 128, (s % 2) * 1024:(s % 2 + 1) * 1024]

    gbuf = [g_a, g_b]
    gamma_use = []
    gamma_n = [0]

    def gamma_pump2(k):
        for _ in range(k):
            i = gamma_n[0]
            if i >= 256:
                return
            gamma_n[0] = i + 1
            buf = gbuf[i % 2]
            w = [("vv", gamma_use[i - 2])] if i >= 2 else []
            dma("scalar", "cq", buf[:, :], gsrc(i), wait=w)
            ld = C["cq"]
            wabs = [("cq", ld)]
            if gamma_use:
                wabs.append(("vv", gamma_use[-1]))
            E("scalar", lambda e, b=buf, col=i: e.activation(
                junk[:, :], b[:, 0:512], ACT.Abs, accum_out=gacc[:, col:col + 1]),
              wait=wabs, inc="sv")
            E("scalar", lambda e, b=buf: e.activation(
                junk[:, :], b[:, 512:1024], ACT.Abs, accum_out=st[:, 38:39]),
              inc="sv")
            sv2 = C["sv"]
            EV(lambda e, col=i: e.tensor_tensor(
                gacc[:, col:col + 1], gacc[:, col:col + 1], st[:, 38:39], ALU.add),
               wait=[("sv", sv2)])
            gamma_use.append(C["vv"])

    gamma_pump = gamma_pump2

    xt_use = [[], ]
    x1t_tuse = [0]

    def emit_ln1_tile(src_ap, dst_sb, dst_col, extra=()):
        w = list(extra) + list(xt_use[0])
        dma("sync", "sq", xt[:, :], src_ap, wait=w)
        ld = C["sq"]
        E("vector", lambda e: e.reduce_sum(st[:, 0:1], xt[:, :], AXX),
          wait=[("sq", ld)], inc="vv")
        E("scalar", lambda e: e.activation(junk[:, :], xt[:, 0:512], ACT.Square,
                                           accum_out=st[:, 1:2]),
          wait=[("sq", ld)], inc="sv")
        for cc in range(1, 4):
            E("scalar", lambda e, cc=cc: e.activation(
                junk[:, :], xt[:, cc * 512:(cc + 1) * 512], ACT.Square,
                accum_out=st[:, 1 + cc:2 + cc]), inc="sv")
        sden, vden = C["sv"], C["vv"]
        EV(lambda e: e.reduce_sum(st[:, 5:6], st[:, 1:5], AXX), wait=[("sv", sden)])
        EV(lambda e: e.tensor_scalar(st[:, 2:3], st[:, 0:1], 1.0 / D, None, ALU.mult),
           wait=[("vv", vden)])
        EV(lambda e: e.tensor_scalar(st[:, 3:4], st[:, 5:6], 1.0 / D, None, ALU.mult))
        EV(lambda e: e.tensor_tensor(st[:, 4:5], st[:, 2:3], st[:, 2:3], ALU.mult))
        EV(lambda e: e.scalar_tensor_tensor(st[:, 5:6], st[:, 3:4], EPS_LN, st[:, 4:5],
                                            ALU.add, ALU.subtract))
        vstat = C["vv"]
        ES(lambda e: e.activation(st[:, 6:7], st[:, 5:6], ACT.Ln), wait=[("vv", vstat)])
        ES(lambda e: e.activation(st[:, 7:8], st[:, 6:7], ACT.Exp, scale=-0.5))
        srd = C["sv"]
        EV(lambda e: e.tensor_scalar(st[:, 8:9], st[:, 2:3], -1.0, None, ALU.mult),
           wait=[("sv", srd)])
        EV(lambda e: e.tensor_tensor(st[:, 9:10], st[:, 8:9], st[:, 7:8], ALU.mult))
        vmr = C["vv"]
        wap = [("vv", vmr)]
        if x1t_tuse[0]:
            wap.append(("tv", x1t_tuse[0]))
        EV(lambda e: e.tensor_scalar(x1t[:, :], xt[:, :], st[:, 7:8], st[:, 9:10],
                                     ALU.mult, ALU.add), wait=wap)
        vap = C["vv"]
        xt_use[0] = [("vv", vap), ("sv", sden)]
        for g in range(4):
            gh = g % 2
            bk_ = 4 + gh
            for j in range(4):
                di = g * 4 + j
                wt = []
                if j == 0:
                    wt = [("vv", vap)] + bwait(bk_)
                E("tensor", lambda e, di=di, gh=gh, j=j: e.matmul(
                    pt_ps[:, gh * 512 + j * 128: gh * 512 + j * 128 + 128],
                    x1t[:, di * 128:(di + 1) * 128], ident[:, :], start=True, stop=True),
                    wait=wt, inc="tv")
            tpd = C["tv"]
            x1t_tuse[0] = tpd
            for j in range(4):
                di = g * 4 + j
                E("vector", lambda e, di=di, gh=gh, j=j, col=dst_col, dst=dst_sb: e.tensor_copy(
                    dst[:, di * (dst.shape[1] // ND) + col: di * (dst.shape[1] // ND) + col + 128],
                    pt_ps[:, gh * 512 + j * 128: gh * 512 + j * 128 + 128]),
                    wait=[("tv", tpd)] if j == 0 else [], inc="vv")
            bset(bk_, ev=[("vv", C["vv"])])
        return C["vv"]

    wsl_tuse = [0]
    x1_tuse = [0]
    vwr = 0

    for half in range(2):
        for i in range(8):
            tile = half * 8 + i
            extra = []
            if half == 0 and i == 0:
                extra = [("sq", consts_sq), ("gq", consts_gq), ("gv", consts_gv)]
            if x1_tuse[0]:
                extra = extra + [("tv", x1_tuse[0])]
            emit_ln1_tile(xkv[tile * 128:(tile + 1) * 128, :], x1kv, i * 128, extra)
            gamma_pump(6)
        x1_ready = C["vv"]
        if DEBUG and half == 0:
            dma("sync", "sq", dbg_x1[:, :], x1kv[:, :], wait=[("vv", x1_ready)])
            dma("sync", "sq", dbg_st[:, :], st[:, :], wait=[("vv", x1_ready)])

        for jq in range(4):
            for dt_ in range(ND):
                wt = [("tv", wsl_tuse[0])] if (dt_ == 0 and wsl_tuse[0]) else []
                dma("gpsimd", "gq", wsl_kv[:, dt_ * 512:(dt_ + 1) * 512],
                    wk[dt_ * 128:(dt_ + 1) * 128, jq * 512:(jq + 1) * 512], wait=wt)
            wk_ld = C["gq"]
            for jj in range(4):
                j = jq * 4 + jj
                for c in range(2):
                    for dt_ in range(ND):
                        wt = []
                        if dt_ == 0:
                            wt = [("gq", wk_ld), ("vv", x1_ready)] + bwait(7)
                        E("tensor", lambda e, jj=jj, c=c, dt_=dt_: e.matmul(
                            sm_ps[:, 0:512],
                            wsl_kv[:, dt_ * 512 + jj * 128: dt_ * 512 + jj * 128 + 128],
                            x1kv[:, dt_ * 1024 + c * 512: dt_ * 1024 + c * 512 + 512],
                            start=(dt_ == 0), stop=(dt_ == ND - 1)), wait=wt, inc="tv")
                    mmk = C["tv"]
                    wsl_tuse[0] = mmk
                    x1_tuse[0] = mmk
                    E("scalar", lambda e, j=j, c=c, half=half: e.activation(
                        k_sb[:, j * T + half * 1024 + c * 512: j * T + half * 1024 + c * 512 + 512],
                        sm_ps[:, 0:512], ACT.Identity, bias=bk_sb[:, j:j + 1]),
                        wait=[("tv", mmk)], inc="sv")
                    bset(7, ev=[("sv", C["sv"])])
            gamma_pump(2)

        for jc in range(4):
            for dt_ in range(ND):
                wt = [("tv", wsl_tuse[0])] if dt_ == 0 else []
                dma("gpsimd", "gq", wsl_kv[:, dt_ * 512:(dt_ + 1) * 512],
                    wv[dt_ * 128:(dt_ + 1) * 128, jc * 512:(jc + 1) * 512], wait=wt)
            wv_ld = C["gq"]
            for tt in range(8):
                tok = half * 8 + tt
                for dt_ in range(ND):
                    wt = []
                    if dt_ == 0:
                        wt = [("gq", wv_ld)] + bwait(tt)
                    E("tensor", lambda e, tt=tt, dt_=dt_: e.matmul(
                        bank_ap(tt),
                        x1kv[:, dt_ * 1024 + tt * 128: dt_ * 1024 + tt * 128 + 128],
                        wsl_kv[:, dt_ * 512:(dt_ + 1) * 512],
                        start=(dt_ == 0), stop=False), wait=wt, inc="tv")
                E("tensor", lambda e, jc=jc, tt=tt: e.matmul(
                    bank_ap(tt), ones_b[:, :], bv_sb[:, jc * 512:(jc + 1) * 512],
                    start=False, stop=True), inc="tv")
                mmv = C["tv"]
                wsl_tuse[0] = mmv
                x1_tuse[0] = mmv
                wt = [("tv", mmv)]
                if vwr:
                    wt.append(("sq", vwr))
                E("vector", lambda e, tt=tt: e.tensor_copy(vbc[:, :], bank_ap(tt)),
                  wait=wt, inc="vv")
                vev = C["vv"]
                bset(tt, ev=[("vv", vev)])
                for hh in range(4):
                    h = jc * 4 + hh
                    dma("sync", "sq",
                        v_d[h * T + tok * 128: h * T + (tok + 1) * 128, :],
                        vbc[:, hh * 128:(hh + 1) * 128], wait=[("vv", vev)] if hh == 0 else [])
                vwr = C["sq"]
            gamma_pump(2)
    vwr_all = vwr

    for half in range(2):
        for i in range(4):
            tile = half * 4 + i
            extra = [("tv", x1_tuse[0])] if x1_tuse[0] else []
            emit_ln1_tile(xq[tile * 128:(tile + 1) * 128, :], x1q, i * 128, extra)
            gamma_pump(6)
        x1q_ready = C["vv"]
        for jq in range(4):
            for dt_ in range(ND):
                wt = [("tv", wsl_tuse[0])] if dt_ == 0 else []
                dma("gpsimd", "gq", wsl_q[:, dt_ * 512:(dt_ + 1) * 512],
                    wq[dt_ * 128:(dt_ + 1) * 128, jq * 512:(jq + 1) * 512], wait=wt)
            wq_ld = C["gq"]
            for jj in range(4):
                j = jq * 4 + jj
                for dt_ in range(ND):
                    wt = []
                    if dt_ == 0:
                        wt = [("gq", wq_ld), ("vv", x1q_ready)] + bwait(7)
                    E("tensor", lambda e, jj=jj, dt_=dt_: e.matmul(
                        sm_ps[:, 0:512],
                        wsl_q[:, dt_ * 512 + jj * 128: dt_ * 512 + jj * 128 + 128],
                        x1q[:, dt_ * 512: dt_ * 512 + 512],
                        start=(dt_ == 0), stop=(dt_ == ND - 1)), wait=wt, inc="tv")
                mmq = C["tv"]
                wsl_tuse[0] = mmq
                x1_tuse[0] = mmq
                E("scalar", lambda e, j=j, half=half: e.activation(
                    q_sb[:, j * TOWN + half * 512: j * TOWN + half * 512 + 512],
                    sm_ps[:, 0:512], ACT.Identity, bias=bq_sb[:, j:j + 1]),
                    wait=[("tv", mmq)], inc="sv")
                bset(7, ev=[("sv", C["sv"])])
            gamma_pump(4)

    gamma_pump(256)
    for c in range(4):
        EV(lambda e, c=c: e.reduce_sum(st[:, 10 + c:11 + c], gacc[:, c * 32:(c + 1) * 32], AXX),
           wait=[("vv", gamma_use[127])] if c == 0 else [])
    for c in range(4):
        EV(lambda e, c=c: e.reduce_sum(st[:, 14 + c:15 + c],
                                       gacc[:, 128 + c * 32:128 + (c + 1) * 32], AXX),
           wait=[("vv", gamma_use[255])] if c == 0 else [])
    vg = C["vv"]
    gp128 = gpart_d[0:1, 0:1024].rearrange("o (p c) -> (o p) c", p=128)
    dma("sync", "sq", gp128, st[:, 10:18], wait=[("vv", vg)])
    gp_wr = C["sq"]
    dma("sync", "sq", g_b[0:1, 0:1024], gpart_d[0:1, 0:1024], wait=[("sq", gp_wr)])
    kst_ld = C["sq"]
    kS = st[0:1, 16:24]
    kC = st[0:1, 24:32]
    kY = st[0:1, 40:48]
    kT = st[0:1, 48:56]
    EV(lambda e: e.tensor_copy(kS, g_b[0:1, 0:8]), wait=[("sq", kst_ld)])
    EV(lambda e: e.tensor_scalar(kC, kS, 0.0, None, ALU.mult))
    for i in range(1, 128):
        xi = g_b[0:1, i * 8:(i + 1) * 8]
        EV(lambda e, xi=xi: e.tensor_tensor(kY, xi, kC, ALU.subtract))
        EV(lambda e: e.tensor_tensor(kT, kS, kY, ALU.add))
        EV(lambda e: e.tensor_tensor(kC, kT, kS, ALU.subtract))
        EV(lambda e: e.tensor_tensor(kC, kC, kY, ALU.subtract))
        EV(lambda e: e.tensor_copy(kS, kT))
    fS = st[0:1, 56:57]
    fC = st[0:1, 57:58]
    fY = st[0:1, 58:59]
    fT = st[0:1, 59:60]
    for g in range(2):
        lanes = ([st[0:1, 16 + g * 4 + k:17 + g * 4 + k] for k in range(4)] +
                 [st[0:1, 24 + g * 4 + k:25 + g * 4 + k] for k in range(4)])
        EV(lambda e, l0=lanes[0]: e.tensor_copy(fS, l0))
        EV(lambda e: e.tensor_scalar(fC, fS, 0.0, None, ALU.mult))
        for x in lanes[1:]:
            EV(lambda e, x=x: e.tensor_tensor(fY, x, fC, ALU.subtract))
            EV(lambda e: e.tensor_tensor(fT, fS, fY, ALU.add))
            EV(lambda e: e.tensor_tensor(fC, fT, fS, ALU.subtract))
            EV(lambda e: e.tensor_tensor(fC, fC, fY, ALU.subtract))
            EV(lambda e: e.tensor_copy(fS, fT))
        EV(lambda e: e.tensor_tensor(fT, fS, fC, ALU.add))
        EV(lambda e, g=g: e.tensor_scalar(gval[:, g:g + 1], fT, 1.0 / NW, None, ALU.mult))
    for g in range(2):
        EV(lambda e, g=g: e.tensor_scalar(st[0:1, 60:61], gval[:, g:g + 1],
                                          EPS_BL, 0.5, ALU.add, ALU.mult))
        EV(lambda e, g=g: e.tensor_scalar(st[0:1, 61:62], gval[:, g:g + 1],
                                          EPS_BL, 2.0 ** -25, ALU.add, ALU.mult))
        EV(lambda e, g=g: e.tensor_tensor(gval[:, 2 + g:3 + g], st[0:1, 60:61],
                                          st[0:1, 61:62], ALU.add))
        EV(lambda e, g=g: e.tensor_tensor(st[0:1, 62:63], gval[:, 2 + g:3 + g],
                                          st[0:1, 60:61], ALU.subtract))
        EV(lambda e, g=g: e.tensor_tensor(gval[:, 4 + g:5 + g], st[0:1, 61:62],
                                          st[0:1, 62:63], ALU.subtract))
        EV(lambda e, g=g: e.tensor_scalar(gval[:, 6 + g:7 + g], gval[:, 4 + g:5 + g],
                                          -1.0, None, ALU.mult))
    vgs = C["vv"]
    for i, sc in enumerate([2, 3, 0, 1, 4, 5, 6, 7]):
        E("tensor", lambda e, i=i, sc=sc: e.matmul(
            sm_ps[:, 2 + i:3 + i], ones_f[:, :], gval[:, sc:sc + 1], start=True, stop=True),
            wait=([("vv", vgs), ("gv", consts_gv)] + bwait(7) if i == 0 else []), inc="tv")
    tbc = C["tv"]
    E("vector", lambda e: e.tensor_copy(thr[:, 0:8], sm_ps[:, 2:10]), wait=[("tv", tbc)], inc="vv")
    thr_done = C["vv"]
    bset(7, ev=[("vv", thr_done)])

    tibuf = [tin_a, tin_b]
    tobuf = [to1, to2]
    tern_out = []
    tern_v = []
    tern_n = [0]

    def tern_pump(kk):
        for _ in range(kk):
            i = tern_n[0]
            if i >= 256:
                return
            tern_n[0] = i + 1
            if i < 128:
                s = i
                src = wfcT[(s // 8) * 128:(s // 8 + 1) * 128, (s % 8) * 1024:(s % 8 + 1) * 1024]
                dst = tfc_d[(s // 8) * 128:(s // 8 + 1) * 128, (s % 8) * 1024:(s % 8 + 1) * 1024]
                tc = 0
            else:
                s = i - 128
                src = wprojT[(s // 2) * 128:(s // 2 + 1) * 128, (s % 2) * 1024:(s % 2 + 1) * 1024]
                dst = tpj_d[(s // 2) * 128:(s // 2 + 1) * 128, (s % 2) * 1024:(s % 2 + 1) * 1024]
                tc = 1
            ib = tibuf[i % 2]
            ob = tobuf[i % 2]
            w = [("vv", tern_v[i - 2])] if i >= 2 else [("vv", thr_done)]
            dma("scalar", "cq", ib[:, :], src, wait=w)
            ld = C["cq"]
            E("vector", lambda e, b=ib, tc=tc: e.tensor_scalar(
                g_a[:, :], b[:, :], thr[:, tc:tc + 1], None, ALU.subtract),
                wait=[("cq", ld), ("vv", thr_done)], inc="vv")
            E("vector", lambda e, tc=tc: e.tensor_scalar(
                tm1[:, :], g_a[:, :], thr[:, 4 + tc:5 + tc], None, ALU.is_gt), inc="vv")
            E("vector", lambda e, b=ib, tc=tc: e.tensor_scalar(
                g_a[:, :], b[:, :], thr[:, tc:tc + 1], None, ALU.add), inc="vv")
            E("vector", lambda e, tc=tc: e.tensor_scalar(
                tm0[:, :], g_a[:, :], thr[:, 6 + tc:7 + tc], None, ALU.is_lt), inc="vv")
            wst = [("gq", tern_out[i - 2])] if i >= 2 else []
            E("vector", lambda e, ob=ob: e.tensor_tensor(
                ob[:, :], tm1[:, :], tm0[:, :], ALU.subtract), wait=wst, inc="vv")
            tern_v.append(C["vv"])
            dma("gpsimd", "gq", dst, ob[:, :], wait=[("vv", tern_v[-1])])
            tern_out.append(C["gq"])

    prev_exp = [("sv", C["sv"])]
    prev_pt = [0]
    prev_avev = [0]
    vstr_use = [0, 0]
    vld_prev = [0, 0]
    pts_use = [0, 0]

    for u in range(8):
        width = (2 * u + 2) * 128
        nch = (width + 511) // 512
        nkt = 2 * u + 2
        for h in range(H):
            idx = u * H + h
            vb = [vs_a, vs_b][idx % 2]
            wt = [("sq", vwr_all)] if idx == 0 else []
            if vstr_use[idx % 2]:
                wt.append(("tv", vstr_use[idx % 2]))
            dma("sync", "sq",
                vb[:, 0:width].rearrange("p (n c) -> p n c", c=128),
                v_d[h * T: h * T + width, :].rearrange("(n p) c -> p n c", p=128),
                wait=wt)
            vld = C["sq"]
            vld_prev[idx % 2] = vld

            for c in range(nch):
                w0 = c * 512
                cw = min(512, width - w0)
                wt = bwait(c) + ([("vv", thr_done)] if idx == 0 else [])
                E("tensor", lambda e, u=u, h=h, w0=w0, cw=cw: e.matmul(
                    s_ps[:, w0:w0 + cw],
                    q_sb[:, h * TOWN + u * 128: h * TOWN + u * 128 + 128],
                    k_sb[:, h * T + w0: h * T + w0 + cw],
                    start=True, stop=True), wait=wt, inc="tv")
            smm = C["tv"]
            E("vector", lambda e, u=u: e.tensor_tensor(
                s_ps[:, 2 * u * 128: 2 * u * 128 + 128],
                s_ps[:, 2 * u * 128: 2 * u * 128 + 128], mska[:, :], ALU.add),
                wait=[("tv", smm)], inc="vv")
            E("vector", lambda e, u=u: e.tensor_tensor(
                s_ps[:, (2 * u + 1) * 128: (2 * u + 1) * 128 + 128],
                s_ps[:, (2 * u + 1) * 128: (2 * u + 1) * 128 + 128], mskb[:, :], ALU.add),
                inc="vv")
            E("vector", lambda e, width=width: e.reduce_max(
                expb[:, 0:1], s_ps[:, 0:width], AXX, negate=True), inc="vv")
            EV(lambda e: e.tensor_scalar(expb[:, 1:2], expb[:, 0:1], SM_SCALE, None, ALU.mult))
            vmax = C["vv"]
            for c in range(nch):
                w0 = c * 512
                cw = min(512, width - w0)
                wt = []
                if c == 0:
                    wt = [("vv", vmax)]
                    if prev_pt[0]:
                        wt.append(("tv", prev_pt[0]))
                E("scalar", lambda e, w0=w0, cw=cw, c=c: e.activation(
                    p_sb[:, w0:w0 + cw], s_ps[:, w0:w0 + cw], ACT.Exp,
                    bias=expb[:, 1:2], scale=SM_SCALE, accum_out=lsum[:, c:c + 1]),
                    wait=wt, inc="sv")
            sexp = C["sv"]
            bset(*range(nch), ev=[("sv", sexp), ("vv", vmax)])
            EV(lambda e, nch=nch: e.reduce_sum(st[:, 12:13], lsum[:, 0:nch], AXX),
               wait=[("sv", sexp)])
            EV(lambda e: e.reciprocal(st[:, 13:14], st[:, 12:13]))
            EV(lambda e, width=width: e.tensor_scalar(
                p_sb[:, 0:width], p_sb[:, 0:width], st[:, 13:14], None, ALU.mult))
            vnorm = C["vv"]
            for g in range((nkt + 3) // 4):
                gh = g % 2
                bk_ = 4 + gh
                blocks = list(range(g * 4, min(g * 4 + 4, nkt)))
                for bi, kt in enumerate(blocks):
                    wt = []
                    if bi == 0:
                        wt = [("vv", vnorm)] + bwait(bk_)
                    E("tensor", lambda e, kt=kt, gh=gh, bi=bi: e.matmul(
                        pt_ps[:, gh * 512 + bi * 128: gh * 512 + bi * 128 + 128],
                        p_sb[:, kt * 128:(kt + 1) * 128], ident[:, :],
                        start=True, stop=True), wait=wt, inc="tv")
                tptg = C["tv"]
                prev_pt[0] = tptg
                wt = [("tv", tptg)]
                if pts_use[gh]:
                    wt.append(("tv", pts_use[gh]))
                E("vector", lambda e, gh=gh, nb=len(blocks): e.tensor_copy(
                    pt_sb[:, gh * 512: gh * 512 + nb * 128],
                    pt_ps[:, gh * 512: gh * 512 + nb * 128]), wait=wt, inc="vv")
                bset(bk_, ev=[("vv", C["vv"])])
                vevg = C["vv"]
                for bi, kt in enumerate(blocks):
                    wt = []
                    if bi == 0:
                        wt = [("vv", vevg), ("sq", vld)]
                        if kt == 0:
                            wt = wt + bwait(6)
                    E("tensor", lambda e, kt=kt, gh=gh, bi=bi, vb=vb,
                        first=(kt == 0), last=(kt == nkt - 1): e.matmul(
                        av_ps[:, 0:128],
                        vb[:, kt * 128:(kt + 1) * 128],
                        pt_sb[:, gh * 512 + bi * 128: gh * 512 + bi * 128 + 128],
                        start=first, stop=last), wait=wt, inc="tv")
                pts_use[gh] = C["tv"]
            tav = C["tv"]
            vstr_use[idx % 2] = tav
            E("vector", lambda e, h=h, u=u: e.tensor_copy(
                attn_sb[:, h * TOWN + u * 128: h * TOWN + u * 128 + 128], av_ps[:, 0:128]),
                wait=[("tv", tav)], inc="vv")
            bset(6, ev=[("vv", C["vv"])])
            tern_pump(2)

    if DEBUG:
        dma("sync", "sq", dbg_k[:, :], k_sb[:, :], wait=[("sv", C["sv"])])
        dma("sync", "sq", dbg_q[:, :], q_sb[:, :], wait=[("sv", C["sv"])])
        dma("sync", "sq", dbg_attn[:, :], attn_sb[:, :], wait=[("vv", C["vv"])])
    tern_pump(256)
    tern_all_gq = C["gq"]
    attn_done_v = C["vv"]
    attn_done_s = C["sv"]
    attn_done_t = C["tv"]

    op_ev = [0] * 8
    wch_tuse = [0, 0]
    xq_vuse = [0]
    for ec in range(4):
        for f in range(ND):
            i = ec * ND + f
            buf = wch_a if i % 2 == 0 else wch_b
            wt = [("tv", wch_tuse[i % 2])] if wch_tuse[i % 2] else [("vv", attn_done_v)]
            dma("gpsimd", "gq", buf[:, :],
                wo[f * 128:(f + 1) * 128, ec * 512:(ec + 1) * 512], wait=wt)
            wo_ld = C["gq"]
            for t in range(8):
                wt = []
                if t == 0:
                    wt = [("gq", wo_ld)]
                if f == 0:
                    wt = wt + bwait(t) + [("vv", attn_done_v)]
                E("tensor", lambda e, f=f, t=t, buf=buf: e.matmul(
                    bank_ap(t),
                    attn_sb[:, f * TOWN + t * 128: f * TOWN + t * 128 + 128],
                    buf[:, :], start=(f == 0), stop=False), wait=wt, inc="tv")
            wch_tuse[i % 2] = C["tv"]
        for t in range(8):
            E("tensor", lambda e, ec=ec, t=t: e.matmul(
                bank_ap(t), ones_b[:, :], bo_sb[:, ec * 512:(ec + 1) * 512],
                start=False, stop=True), inc="tv")
            topt = C["tv"]
            wt = [("vv", xq_vuse[0])] if xq_vuse[0] else []
            dma("sync", "sq", xq_c[:, :],
                xq[t * 128:(t + 1) * 128, ec * 512:(ec + 1) * 512], wait=wt)
            xq_ld = C["sq"]
            E("vector", lambda e, t=t, ec=ec: e.tensor_tensor(
                xnew_sb[:, t * D + ec * 512: t * D + ec * 512 + 512],
                bank_ap(t), xq_c[:, :], ALU.add),
                wait=[("tv", topt), ("sq", xq_ld)], inc="vv")
            op_ev[t] = C["vv"]
            xq_vuse[0] = C["vv"]
            bset(t, ev=[("vv", C["vv"])])

    xnew_wr = [0] * 8
    xhat_dma = [0]
    prev_tv = [C["vv"]]
    for t in range(8):
        dma("sync", "sq", xnew_d[t * 128:(t + 1) * 128, :],
            xnew_sb[:, t * D: t * D + D], wait=[("vv", op_ev[t])])
        xnew_wr[t] = C["sq"]
        E("vector", lambda e, t=t: e.reduce_sum(st[:, 16:17], xnew_sb[:, t * D: t * D + D], AXX),
          wait=[("vv", op_ev[t])], inc="vv")
        for cc in range(4):
            E("scalar", lambda e, t=t, cc=cc: e.activation(
                junk[:, :], xnew_sb[:, t * D + cc * 512: t * D + (cc + 1) * 512], ACT.Square,
                accum_out=st[:, 1 + cc:2 + cc]),
                wait=[("vv", prev_tv[0])] if cc == 0 else [], inc="sv")
        s0 = C["sv"]
        v0 = C["vv"]
        EV(lambda e: e.reduce_sum(st[:, 17:18], st[:, 1:5], AXX), wait=[("sv", s0)])
        EV(lambda e: e.tensor_scalar(st[:, 18:19], st[:, 16:17], 1.0 / D, None, ALU.mult),
           wait=[("vv", v0)])
        EV(lambda e: e.tensor_scalar(st[:, 19:20], st[:, 17:18], 1.0 / D, None, ALU.mult))
        EV(lambda e: e.tensor_tensor(st[:, 20:21], st[:, 18:19], st[:, 18:19], ALU.mult))
        EV(lambda e: e.scalar_tensor_tensor(st[:, 21:22], st[:, 19:20], EPS_LN,
                                            st[:, 20:21], ALU.add, ALU.subtract))
        vz2 = C["vv"]
        ES(lambda e: e.activation(st[:, 22:23], st[:, 21:22], ACT.Ln), wait=[("vv", vz2)])
        ES(lambda e: e.activation(st[:, 23:24], st[:, 22:23], ACT.Exp, scale=-0.5))
        sr2 = C["sv"]
        EV(lambda e, t=t: e.tensor_scalar(w2[:, :], xnew_sb[:, t * D: t * D + D],
                                          st[:, 18:19], None, ALU.subtract), wait=[("vv", vz2)])
        vw2 = C["vv"]
        EV(lambda e: e.reduce_sum(st[:, 24:25], w2[:, :], AXX))
        for cc in range(4):
            E("scalar", lambda e, cc=cc: e.activation(
                junk[:, :], w2[:, cc * 512:(cc + 1) * 512], ACT.Square,
                accum_out=st[:, 1 + cc:2 + cc]),
                wait=[("vv", vw2), ("sv", sr2)] if cc == 0 else [], inc="sv")
        sw2 = C["sv"]
        EV(lambda e: e.reduce_sum(st[:, 25:26], st[:, 1:5], AXX), wait=[("sv", sw2)])
        EV(lambda e: e.tensor_scalar(st[:, 26:27], st[:, 24:25], 1.0 / D, None, ALU.mult))
        EV(lambda e: e.tensor_scalar(w2[:, :], w2[:, :], st[:, 26:27], None, ALU.subtract))
        EV(lambda e: e.reduce_max(st[:, 27:28], w2[:, :], AXX, apply_absolute_value=True))
        vA = C["vv"]
        EV(lambda e: e.tensor_scalar(st[:, 28:29], st[:, 25:26], 1.0 / D, None, ALU.mult))
        EV(lambda e: e.tensor_tensor(st[:, 29:30], st[:, 26:27], st[:, 26:27], ALU.mult))
        EV(lambda e: e.tensor_tensor(st[:, 30:31], st[:, 28:29], st[:, 29:30], ALU.subtract))
        EV(lambda e: e.tensor_tensor(st[:, 31:32], st[:, 23:24], st[:, 23:24], ALU.mult))
        EV(lambda e: e.tensor_tensor(st[:, 32:33], st[:, 30:31], st[:, 31:32], ALU.mult))
        EV(lambda e: e.tensor_scalar(st[:, 33:34], st[:, 32:33], EPS_BL, None, ALU.add))
        vz3 = C["vv"]
        ES(lambda e: e.activation(st[:, 34:35], st[:, 33:34], ACT.Ln), wait=[("vv", vz3)])
        ES(lambda e: e.activation(st[:, 35:36], st[:, 34:35], ACT.Exp, scale=-0.5))
        sr3 = C["sv"]
        EV(lambda e: e.tensor_tensor(st[:, 36:37], st[:, 23:24], st[:, 27:28], ALU.mult),
           wait=[("sv", sr3)])
        EV(lambda e: e.tensor_scalar(st[:, 37:38], st[:, 36:37], st[:, 35:36], 1.0 / QB,
                                     ALU.mult, ALU.mult))
        EV(lambda e, t=t: e.tensor_tensor(sfc[:, t:t + 1], st[:, 37:38], thr[:, 2:3], ALU.mult))
        EV(lambda e: e.reciprocal(st[:, 39:40], st[:, 27:28]))
        EV(lambda e: e.tensor_scalar(st[:, 39:40], st[:, 39:40], QB, None, ALU.mult))
        wt = [("sq", xhat_dma[0])] if xhat_dma[0] else []
        EV(lambda e: e.tensor_scalar(xhat_t[:, :], w2[:, :], st[:, 39:40], None, ALU.mult),
           wait=wt)
        EV(lambda e: e.tensor_scalar(xhat_t[:, :], xhat_t[:, :], CLIP_HI, CLIP_LO,
                                     ALU.min, ALU.max))
        vxh = C["vv"]
        prev_tv[0] = vxh
        dma("sync", "sq", xhat_d[t * 128:(t + 1) * 128, :], xhat_t[:, :], wait=[("vv", vxh)])
        xhat_dma[0] = C["sq"]

    xhat_wr = C["sq"]
    sfc_done = C["vv"]
    for dt_ in range(ND):
        wt = [("sq", xhat_wr), ("tv", attn_done_t)] if dt_ == 0 else []
        dmat(q_sb[:, dt_ * 1024:(dt_ + 1) * 1024],
             xhat_d[:, dt_ * 128:(dt_ + 1) * 128], wait=wt)
    xfm_ld = C["sq"]

    fc_ev = [0] * 8
    fc_wuse = [0, 0]
    for oc in range(16):
        for dt_ in range(ND):
            i = oc * ND + dt_
            buf = wch_a if i % 2 == 0 else wch_b
            wt = [("tv", fc_wuse[i % 2])] if fc_wuse[i % 2] else []
            if i < 2:
                wt = wt + [("sq", xfm_ld), ("gq", tern_all_gq)]
            dma("sync", "sq", buf[:, :],
                tfc_d[dt_ * 128:(dt_ + 1) * 128, oc * 512:(oc + 1) * 512], wait=wt)
            tld = C["sq"]
            for t in range(8):
                wt = []
                if t == 0:
                    wt = [("sq", tld)]
                if dt_ == 0:
                    wt = wt + (bwait(t) if fc_ev[t] else bwait(t) + [("vv", sfc_done)])
                E("tensor", lambda e, dt_=dt_, t=t, buf=buf: e.matmul(
                    bank_ap(t),
                    q_sb[:, dt_ * 1024 + t * 128: dt_ * 1024 + t * 128 + 128],
                    buf[:, :], start=(dt_ == 0), stop=(dt_ == ND - 1)), wait=wt, inc="tv")
            fc_wuse[i % 2] = C["tv"]
        tfc_mm = C["tv"]
        for t in range(8):
            E("scalar", lambda e, t=t, oc=oc: e.activation(
                h_ap(t, oc * 512, 512), bank_ap(t), ACT.Gelu, scale=sfc[:, t:t + 1],
                accum_out=gacc[:, t * 16 + oc: t * 16 + oc + 1]),
                wait=[("tv", tfc_mm)], inc="sv")
            fc_ev[t] = C["sv"]
            bset(t, ev=[("sv", C["sv"])])

    hhat_dma = [0]
    prev_svq = [C["sv"]]
    for t in range(8):
        EV(lambda e, t=t: e.reduce_sum(st[:, 16:17], gacc[:, t * 16:(t + 1) * 16], AXX),
           wait=[("sv", fc_ev[t])])
        EV(lambda e: e.tensor_scalar(st[:, 18:19], st[:, 16:17], 1.0 / DFF, None, ALU.mult))
        vm = C["vv"]
        for oc in range(16):
            wt = []
            if oc == 0:
                wt = [("sv", fc_ev[t]), ("vv", prev_svq[0])]
            E("scalar", lambda e, t=t, oc=oc: e.activation(
                junk[:, :], h_ap(t, oc * 512, 512), ACT.Square,
                accum_out=st[:, 40 + oc:41 + oc]), wait=wt, inc="sv")
        ssq = C["sv"]
        EV(lambda e: e.reduce_sum(st[:, 17:18], st[:, 40:56], AXX), wait=[("sv", ssq)])
        EV(lambda e: e.tensor_scalar(st[:, 19:20], st[:, 17:18], 1.0 / DFF, None, ALU.mult))
        EV(lambda e: e.tensor_tensor(st[:, 20:21], st[:, 18:19], st[:, 18:19], ALU.mult))
        EV(lambda e: e.tensor_tensor(st[:, 21:22], st[:, 19:20], st[:, 20:21], ALU.subtract))
        EV(lambda e: e.tensor_scalar(st[:, 21:22], st[:, 21:22], EPS_BL, None, ALU.add))
        vz = C["vv"]
        for oc in range(16):
            EV(lambda e, t=t, oc=oc: e.tensor_scalar(
                uq[:, 0:512], h_ap(t, oc * 512, 512), st[:, 18:19], None, ALU.subtract),
                wait=[("vv", vm)] if oc == 0 else [])
            EV(lambda e, oc=oc: e.reduce_max(st[:, 40 + oc:41 + oc], uq[:, 0:512], AXX,
                                             apply_absolute_value=True))
        EV(lambda e: e.reduce_max(st[:, 27:28], st[:, 40:56], AXX))
        vA = C["vv"]
        ES(lambda e: e.activation(st[:, 34:35], st[:, 21:22], ACT.Ln), wait=[("vv", vz)])
        ES(lambda e: e.activation(st[:, 35:36], st[:, 34:35], ACT.Exp, scale=-0.5))
        sr = C["sv"]
        prev_svq[0] = C["vv"]
        EV(lambda e: e.tensor_scalar(st[:, 37:38], st[:, 27:28], 1.0 / QB, st[:, 35:36],
                                     ALU.mult, ALU.mult), wait=[("sv", sr), ("vv", vA)])
        EV(lambda e, t=t: e.tensor_tensor(spj[:, t:t + 1], st[:, 37:38], thr[:, 3:4], ALU.mult))
        EV(lambda e: e.reciprocal(st[:, 39:40], st[:, 27:28]), wait=[("vv", vA)])
        EV(lambda e: e.tensor_scalar(st[:, 39:40], st[:, 39:40], QB, None, ALU.mult))
        hq = 0
        for oc in range(16):
            wt = []
            if oc == 0 and hhat_dma[0]:
                wt.append(("sq", hhat_dma[0]))
            EV(lambda e, t=t, oc=oc: e.tensor_scalar(
                uq[:, 0:512], h_ap(t, oc * 512, 512), st[:, 18:19], None, ALU.subtract),
                wait=wt)
            EV(lambda e: e.tensor_scalar(uq[:, 512:1024], uq[:, 0:512], st[:, 39:40],
                                         None, ALU.mult))
            EV(lambda e, t=t, oc=oc: e.tensor_scalar(
                h_ap(t, oc * 512, 512), uq[:, 512:1024], CLIP_HI, CLIP_LO, ALU.min, ALU.max))
            hq = C["vv"]
        dma("sync", "sq", hhat_d[t * 128:(t + 1) * 128, :], h_ap(t, 0, DFF), wait=[("vv", hq)])
        hhat_dma[0] = C["sq"]
    hhat_wr = C["sq"]
    spj_done = C["vv"]

    for o in range(NO):
        wt = [("sq", hhat_wr)] if o == 0 else []
        dmat(hfm_ap(o, 0, 1024), hhat_d[:, o * 128:(o + 1) * 128], wait=wt)
    hfm_ld = C["sq"]

    pj_ev = [0] * 8
    pj_wuse = [0, 0]
    xn_vuse = [0]
    out_duse = [0]
    for ec in range(4):
        for o in range(NO):
            i = ec * NO + o
            buf = wch_a if i % 2 == 0 else wch_b
            wt = [("tv", pj_wuse[i % 2])] if pj_wuse[i % 2] else [("sq", hfm_ld)]
            dma("sync", "sq", buf[:, :],
                tpj_d[o * 128:(o + 1) * 128, ec * 512:(ec + 1) * 512], wait=wt)
            tld = C["sq"]
            for t in range(8):
                wt = []
                if t == 0:
                    wt = [("sq", tld)]
                if o == 0:
                    wt = wt + bwait(t) + ([("sq", hfm_ld)] if ec == 0 else [])
                E("tensor", lambda e, o=o, t=t, buf=buf: e.matmul(
                    bank_ap(t), hfm_ap(o, t * 128, 128),
                    buf[:, :], start=(o == 0), stop=(o == NO - 1)), wait=wt, inc="tv")
            pj_wuse[i % 2] = C["tv"]
        tpj_mm = C["tv"]
        for t in range(8):
            wt = [("sq", xnew_wr[t])]
            if xn_vuse[0]:
                wt.append(("vv", xn_vuse[0]))
            dma("sync", "sq", xq_c[:, :],
                xnew_d[t * 128:(t + 1) * 128, ec * 512:(ec + 1) * 512], wait=wt)
            xld = C["sq"]
            wt = [("tv", tpj_mm), ("sq", xld)]
            if t == 0 and ec == 0:
                wt.append(("vv", spj_done))
            if out_duse[0]:
                wt.append(("sq", out_duse[0]))
            E("vector", lambda e, t=t: e.scalar_tensor_tensor(
                out_c[:, :], bank_ap(t), spj[:, t:t + 1], xq_c[:, :], ALU.mult, ALU.add),
                wait=wt, inc="vv")
            pj_ev[t] = C["vv"]
            xn_vuse[0] = C["vv"]
            bset(t, ev=[("vv", C["vv"])])
            dma("sync", "sq", out[t * 128:(t + 1) * 128, ec * 512:(ec + 1) * 512], out_c[:, :],
                wait=[("vv", pj_ev[t])])
            out_duse[0] = C["sq"]

    final_sq = C["sq"]

    with nc.Block() as blk:
        def mk(eng_name):
            def body(e):
                for waits, fn, incs, n in OPS[eng_name]:
                    for (s, v) in waits:
                        if v > 0:
                            e.wait_ge(SEM[s], v)
                    ins = fn(e)
                    if incs is not None:
                        ins.then_inc(SEM[incs], n)
                if eng_name == "sync":
                    e.wait_ge(SEM["sq"], final_sq)
            return body

        blk.sync(mk("sync"))
        blk.gpsimd(mk("gpsimd"))
        blk.tensor(mk("tensor"))
        blk.vector(mk("vector"))
        blk.scalar(mk("scalar"))

    for cm in reversed(mgr):
        cm.__exit__(None, None, None)
    return nc


_CACHE = {}


def kernel(L0_attn_query, L0_attn_w_q, L0_attn_b_q, L0_attn_w_k, L0_attn_b_k,
           L0_attn_w_v, L0_attn_b_v, L0_attn_w_o, L0_attn_b_o, L0_ffn_w_fc, L0_ffn_w_proj):
    x = np.asarray(L0_attn_query, dtype=np.float32)
    wq_ = np.ascontiguousarray(np.asarray(L0_attn_w_q, np.float32))
    wk_ = np.ascontiguousarray(np.asarray(L0_attn_w_k, np.float32))
    wv_ = np.ascontiguousarray(np.asarray(L0_attn_w_v, np.float32))
    wo_ = np.ascontiguousarray(np.asarray(L0_attn_w_o, np.float32))
    bq_ = np.ascontiguousarray(np.asarray(L0_attn_b_q, np.float32).reshape(16, 128).T)
    bk_ = np.ascontiguousarray(np.asarray(L0_attn_b_k, np.float32).reshape(16, 128).T)
    bv_ = np.ascontiguousarray(np.asarray(L0_attn_b_v, np.float32).reshape(1, D))
    bo_ = np.ascontiguousarray(np.asarray(L0_attn_b_o, np.float32).reshape(1, D))
    wfcT_ = np.ascontiguousarray(np.asarray(L0_ffn_w_fc, np.float32).T)
    wprojT_ = np.ascontiguousarray(np.asarray(L0_ffn_w_proj, np.float32).T)
    ident_ = np.eye(128, dtype=ml_dtypes.bfloat16)

    tri = np.triu(np.full((128, 128), -1e10, np.float32), k=1)
    mask_sets = {
        0: (np.zeros((128, 128), np.float32), tri),
        1: (tri, np.full((128, 128), -1e10, np.float32)),
    }

    if "nc" not in _CACHE:
        _CACHE["nc"] = build_program()
    nc = _CACHE["nc"]

    in_maps = []
    row_sets = []
    for c in range(8):
        b, p = c // 2, c % 2
        qtiles = [2 * u + 1 - p for u in range(8)]
        rows = np.concatenate([np.arange(qt * 128, qt * 128 + 128) for qt in qtiles])
        row_sets.append((b, rows))
        ma, mb = mask_sets[p]
        in_maps.append({
            "xkv": np.ascontiguousarray(x[b]),
            "xq": np.ascontiguousarray(x[b][rows]),
            "wq": wq_, "wk": wk_, "wv": wv_, "wo": wo_,
            "bq": bq_, "bk": bk_, "bv": bv_, "bo": bo_,
            "wfcT": wfcT_, "wprojT": wprojT_,
            "maska": ma, "maskb": mb, "ident": ident_,
        })

    res = run_bass_kernel_spmd(nc, in_maps, core_ids=list(range(8)))
    outp = np.empty((B, T, D), np.float32)
    for c in range(8):
        b, rows = row_sets[c]
        outp[b][rows] = res.results[c]["out"]
    return outp


# version 09
# speedup vs baseline: 1.0116x; 1.0116x over previous
import math
import numpy as np
import ml_dtypes
import concourse.bass as bass
import concourse.mybir as mybir
from concourse.bass_utils import run_bass_kernel_spmd

F32 = mybir.dt.float32
BF16 = mybir.dt.bfloat16
U8 = mybir.dt.uint8
ALU = mybir.AluOpType
ACT = mybir.ActivationFunctionType
AXX = mybir.AxisListType.X

B, T, D = 4, 2048, 2048
H, DH = 16, 128
DFF = 8192
NT = 16
NOWN = 8
TOWN = NOWN * 128
ND = D // 128
NO = DFF // 128

SM_SCALE = 1.0 / math.sqrt(128.0)
EPS_LN = 1e-5
EPS_BL = 1e-6
QB = 128.0
CLIP_HI = 128.0 - 1e-6
CLIP_LO = -128.0 + 1e-6
NW = float(DFF * D)

ARENA = 212832


DEBUG = False


def build_program():
    nc = bass.Bass("TRN2", target_bir_lowering=False)
    ikind = "ExternalOutput" if DEBUG else "Internal"

    xkv = nc.dram_tensor("xkv", [T, D], F32, kind="ExternalInput")
    xq = nc.dram_tensor("xq", [TOWN, D], F32, kind="ExternalInput")
    wq = nc.dram_tensor("wq", [D, D], F32, kind="ExternalInput")
    wk = nc.dram_tensor("wk", [D, D], F32, kind="ExternalInput")
    wv = nc.dram_tensor("wv", [D, D], F32, kind="ExternalInput")
    wo = nc.dram_tensor("wo", [D, D], F32, kind="ExternalInput")
    bq = nc.dram_tensor("bq", [128, ND], F32, kind="ExternalInput")
    bk = nc.dram_tensor("bk", [128, ND], F32, kind="ExternalInput")
    bv = nc.dram_tensor("bv", [1, D], F32, kind="ExternalInput")
    bo = nc.dram_tensor("bo", [1, D], F32, kind="ExternalInput")
    wfcT = nc.dram_tensor("wfcT", [D, DFF], F32, kind="ExternalInput")
    wprojT = nc.dram_tensor("wprojT", [DFF, D], F32, kind="ExternalInput")
    maska = nc.dram_tensor("maska", [128, 128], F32, kind="ExternalInput")
    maskb = nc.dram_tensor("maskb", [128, 128], F32, kind="ExternalInput")
    ident_in = nc.dram_tensor("ident", [128, 128], BF16, kind="ExternalInput")
    out = nc.dram_tensor("out", [TOWN, D], F32, kind="ExternalOutput")

    tfc_d = nc.dram_tensor("tfc_d", [D, DFF], BF16, kind=ikind)
    tpj_d = nc.dram_tensor("tpj_d", [DFF, D], BF16, kind=ikind)
    v_d = nc.dram_tensor("v_d", [H * T, DH], BF16, kind=ikind)
    xnew_d = nc.dram_tensor("xnew_d", [TOWN, D], F32, kind=ikind)
    xhat_d = nc.dram_tensor("xhat_d", [TOWN, D], BF16, kind=ikind)
    hhat_d = nc.dram_tensor("hhat_d", [TOWN, DFF], BF16, kind=ikind)
    gpart_d = nc.dram_tensor("gpart_d", [1, 1024], F32, kind=ikind)
    if DEBUG:
        dbg_x1 = nc.dram_tensor("dbg_x1", [128, ND * 1024], BF16, kind="ExternalOutput")
        dbg_k = nc.dram_tensor("dbg_k", [128, H * T], BF16, kind="ExternalOutput")
        dbg_q = nc.dram_tensor("dbg_q", [128, H * TOWN], BF16, kind="ExternalOutput")
        dbg_st = nc.dram_tensor("dbg_st", [128, 64], F32, kind="ExternalOutput")
        dbg_attn = nc.dram_tensor("dbg_attn", [128, H * TOWN], BF16, kind="ExternalOutput")

    mgr = []
    arena_cm = nc.sbuf_tensor("arena", [128, ARENA], U8)
    arena = arena_cm.__enter__()
    mgr.append(arena_cm)
    arena_base = nc.lookup_mloc(arena).addr

    def place(name, shape, dtype, off):
        h = nc.virtual_sbuf_tensor(name, shape, dtype)
        nc._tensor(h.name, h.shape, h.dtype, type="SB")
        mloc = nc.lookup_mloc(h)
        mloc.allocated = True
        mloc.addr = arena_base + off
        return h

    ident = place("identb", [128, 128], BF16, 0)
    mska = place("mska", [128, 128], F32, 256)
    mskb = place("mskb", [128, 128], F32, 768)
    bq_sb = place("bq_sb", [128, ND], F32, 1280)
    bk_sb = place("bk_sb", [128, ND], F32, 1344)
    ones_f = place("ones_f", [1, 128], F32, 1408)
    ones_b = place("ones_b", [1, 128], BF16, 1920)
    ones_c = place("ones_c", [128, 1], F32, 2176)
    st = place("st", [128, 64], F32, 2208)
    gval = place("gval", [1, 8], F32, 2464)
    thr = place("thr", [128, 8], F32, 2496)
    sfc = place("sfc", [128, 8], F32, 2528)
    spj = place("spj", [128, 8], F32, 2560)
    expb = place("expb", [128, 2], F32, 2592)
    lsum = place("lsum", [128, 4], F32, 2624)
    gacc = place("gacc", [128, 256], F32, 2656)
    junk = place("junk", [128, 512], F32, 3712)
    wch_a = place("wch_a", [128, 512], BF16, 5760)
    wch_b = place("wch_b", [128, 512], BF16, 6784)
    xq_c = place("xq_c", [128, 512], F32, 7808)
    out_c = place("out_c", [128, 512], F32, 9856)
    uq = place("uq", [128, 1024], F32, 11904)
    xhat_t = place("xhat_t", [128, 2048], BF16, 16000)
    x1t = place("x1t", [128, 2048], BF16, 20096)
    bv_sb = place("bv_sb", [1, 2048], BF16, 24192)
    bo_sb = place("bo_sb", [1, 2048], BF16, 28288)
    xt = place("xt", [128, 2048], F32, 32384)
    p_sb = place("p_sb", [128, 2048], BF16, 40576)
    pt_sb = place("pt_sb", [128, 1024], BF16, 44672)
    vbc = place("vbc", [128, 512], BF16, 46720)
    P_END = 47744
    A1 = P_END
    A2 = A1 + 65536
    A3 = A2 + 32768
    TRS = A3 + 32768
    k_sb = place("k_sb", [128, H * T], BF16, A1)
    xnew_sb = place("xnew_sb", [128, NOWN * D], F32, A1)
    h_lo = place("h_lo", [128, 4 * DFF], BF16, A1)
    x1kv = place("x1kv", [128, ND * 1024], BF16, A2)
    q_sb = place("q_sb", [128, H * TOWN], BF16, A2)
    wsl_kv = place("wsl_kv", [128, ND * 512], BF16, A3)
    x1q = place("x1q", [128, ND * 512], BF16, A3)
    wsl_q = place("wsl_q", [128, ND * 512], BF16, A3 + 16384)
    attn_sb = place("attn_sb", [128, H * TOWN], BF16, A3)
    h_hi = place("h_hi", [128, 4 * DFF], BF16, A3)
    g_a = place("g_a", [128, 1024], F32, TRS)
    g_b = place("g_b", [128, 1024], F32, TRS + 4096)
    w2 = place("w2", [128, 2048], F32, TRS)
    tin_a = place("tin_a", [128, 1024], F32, TRS + 8192)
    tin_b = place("tin_b", [128, 1024], F32, TRS + 12288)
    tm1 = place("tm1", [128, 1024], BF16, TRS + 16384)
    tm0 = place("tm0", [128, 1024], BF16, TRS + 18432)
    to1 = place("to1", [128, 1024], BF16, TRS + 20480)
    to2 = place("to2", [128, 1024], BF16, TRS + 22528)
    vs_a = place("vs_a", [128, 2048], BF16, TRS + 24576)
    vs_b = place("vs_b", [128, 2048], BF16, TRS + 28672)
    assert TRS + 32768 <= ARENA
    assert A3 + 65536 <= ARENA

    def h_ap(t, c0, w):
        base = h_lo if t < 4 else h_hi
        tt = t % 4
        return base[:, tt * DFF + c0: tt * DFF + c0 + w]

    def hfm_ap(o, c0, w):
        base = h_lo if o < 32 else h_hi
        oo = o % 32
        return base[:, oo * 1024 + c0: oo * 1024 + c0 + w]

    def psum(name, shape, dt):
        cm = nc.psum_tensor(name, shape, dt)
        h = cm.__enter__()
        mgr.append(cm)
        return h

    s_ps = psum("s_ps", [128, 2048], F32)
    pt_ps = psum("pt_ps", [128, 1024], F32)
    av_ps = psum("av_ps", [128, 512], F32)
    sm_ps = psum("sm_ps", [128, 512], F32)

    def bank_ap(b, w=512):
        t = [s_ps, s_ps, s_ps, s_ps, pt_ps, pt_ps, av_ps, sm_ps][b]
        off = [0, 512, 1024, 1536, 0, 512, 0, 0][b]
        return t[:, off: off + w]

    sem_names = ["sq", "gq", "cq", "tv", "vv", "sv", "gv"]
    SEM = {}
    for s in sem_names:
        cm = nc.semaphore(s)
        SEM[s] = cm.__enter__()
        mgr.append(cm)
    C = {s: 0 for s in sem_names}
    OPS = {"sync": [], "gpsimd": [], "tensor": [], "vector": [], "scalar": []}
    bank_ev = {i: [] for i in range(8)}

    def E(eng, fn, wait=(), inc=None, n=1):
        OPS[eng].append((list(wait), fn, inc, n))
        if inc is not None:
            C[inc] += n
        return C[inc] if inc else None

    def VS():
        return [("vv", C["vv"])]

    def SS():
        return [("sv", C["sv"])]

    def EV(fn, wait=(), inc="vv"):
        return E("vector", fn, wait=list(wait) + [("vv", C["vv"])], inc=inc)

    def ES(fn, wait=(), inc="sv"):
        return E("scalar", fn, wait=list(wait) + [("sv", C["sv"])], inc=inc)

    def dma(eng, sem, dst, src, wait=()):
        E(eng, lambda e, d=dst, s=src: e.dma_start(out=d, in_=s), wait=wait, inc=sem, n=16)

    def dmat(dst, src, wait=()):
        E("sync", lambda e, d=dst, s=src: e.dma_start(out=d, in_=s, transpose=True),
          wait=wait, inc="sq", n=16)

    def bwait(*banks):
        w = []
        for b in banks:
            for p in bank_ev[b]:
                if p not in w:
                    w.append(p)
        return w

    def bset(*banks, ev):
        for b in banks:
            bank_ev[b] = list(ev)

    dma("sync", "sq", ident[:, :], ident_in[:, :])
    dma("sync", "sq", mska[:, :], maska[:, :])
    dma("sync", "sq", mskb[:, :], maskb[:, :])
    dma("sync", "sq", bq_sb[:, :], bq[:, :])
    dma("sync", "sq", bk_sb[:, :], bk[:, :])
    dma("gpsimd", "gq", bv_sb[:, :], bv[:, :])
    dma("gpsimd", "gq", bo_sb[:, :], bo[:, :])
    consts_sq = C["sq"]
    consts_gq = C["gq"]
    E("gpsimd", lambda e: e.memset(ones_f[:, :], 1.0))
    E("gpsimd", lambda e: e.memset(ones_b[:, :], 1.0))
    E("gpsimd", lambda e: e.memset(ones_c[:, :], 1.0), inc="gv")
    consts_gv = C["gv"]

    def gsrc(i):
        if i < 128:
            s = i
            return wfcT[(s // 8) * 128:(s // 8 + 1) * 128, (s % 8) * 1024:(s % 8 + 1) * 1024]
        s = i - 128
        return wprojT[(s // 2) * 128:(s // 2 + 1) * 128, (s % 2) * 1024:(s % 2 + 1) * 1024]

    gbuf = [g_a, g_b]
    gamma_use = []
    gamma_n = [0]

    def gamma_pump2(k):
        for _ in range(k):
            i = gamma_n[0]
            if i >= 256:
                return
            gamma_n[0] = i + 1
            buf = gbuf[i % 2]
            w = [("vv", gamma_use[i - 2])] if i >= 2 else []
            w.append(("cq", C["cq"]))
            dma("scalar", "cq", buf[:, :], gsrc(i), wait=w)
            ld = C["cq"]
            wabs = [("cq", ld)]
            if gamma_use:
                wabs.append(("vv", gamma_use[-1]))
            E("scalar", lambda e, b=buf, col=i: e.activation(
                junk[:, :], b[:, 0:512], ACT.Abs, accum_out=gacc[:, col:col + 1]),
              wait=wabs, inc="sv")
            E("scalar", lambda e, b=buf: e.activation(
                junk[:, :], b[:, 512:1024], ACT.Abs, accum_out=st[:, 38:39]),
              inc="sv")
            sv2 = C["sv"]
            EV(lambda e, col=i: e.tensor_tensor(
                gacc[:, col:col + 1], gacc[:, col:col + 1], st[:, 38:39], ALU.add),
               wait=[("sv", sv2)])
            gamma_use.append(C["vv"])

    gamma_pump = gamma_pump2

    xt_use = [[], ]
    x1t_tuse = [0]

    def emit_ln1_tile(src_ap, dst_sb, dst_col, extra=()):
        w = list(extra) + list(xt_use[0])
        dma("sync", "sq", xt[:, :], src_ap, wait=w)
        ld = C["sq"]
        E("vector", lambda e: e.reduce_sum(st[:, 0:1], xt[:, :], AXX),
          wait=[("sq", ld)], inc="vv")
        E("scalar", lambda e: e.activation(junk[:, :], xt[:, 0:512], ACT.Square,
                                           accum_out=st[:, 1:2]),
          wait=[("sq", ld)], inc="sv")
        for cc in range(1, 4):
            E("scalar", lambda e, cc=cc: e.activation(
                junk[:, :], xt[:, cc * 512:(cc + 1) * 512], ACT.Square,
                accum_out=st[:, 1 + cc:2 + cc]), inc="sv")
        sden, vden = C["sv"], C["vv"]
        EV(lambda e: e.reduce_sum(st[:, 5:6], st[:, 1:5], AXX), wait=[("sv", sden)])
        EV(lambda e: e.tensor_scalar(st[:, 2:3], st[:, 0:1], 1.0 / D, None, ALU.mult),
           wait=[("vv", vden)])
        EV(lambda e: e.tensor_scalar(st[:, 3:4], st[:, 5:6], 1.0 / D, None, ALU.mult))
        EV(lambda e: e.tensor_tensor(st[:, 4:5], st[:, 2:3], st[:, 2:3], ALU.mult))
        EV(lambda e: e.scalar_tensor_tensor(st[:, 5:6], st[:, 3:4], EPS_LN, st[:, 4:5],
                                            ALU.add, ALU.subtract))
        vstat = C["vv"]
        ES(lambda e: e.activation(st[:, 6:7], st[:, 5:6], ACT.Ln), wait=[("vv", vstat)])
        ES(lambda e: e.activation(st[:, 7:8], st[:, 6:7], ACT.Exp, scale=-0.5))
        srd = C["sv"]
        EV(lambda e: e.tensor_scalar(st[:, 8:9], st[:, 2:3], -1.0, None, ALU.mult),
           wait=[("sv", srd)])
        EV(lambda e: e.tensor_tensor(st[:, 9:10], st[:, 8:9], st[:, 7:8], ALU.mult))
        vmr = C["vv"]
        wap = [("vv", vmr)]
        if x1t_tuse[0]:
            wap.append(("tv", x1t_tuse[0]))
        EV(lambda e: e.tensor_scalar(x1t[:, :], xt[:, :], st[:, 7:8], st[:, 9:10],
                                     ALU.mult, ALU.add), wait=wap)
        vap = C["vv"]
        xt_use[0] = [("vv", vap), ("sv", sden)]
        for g in range(4):
            gh = g % 2
            bk_ = 4 + gh
            for j in range(4):
                di = g * 4 + j
                wt = []
                if j == 0:
                    wt = [("vv", vap)] + bwait(bk_)
                E("tensor", lambda e, di=di, gh=gh, j=j: e.matmul(
                    pt_ps[:, gh * 512 + j * 128: gh * 512 + j * 128 + 128],
                    x1t[:, di * 128:(di + 1) * 128], ident[:, :], start=True, stop=True),
                    wait=wt, inc="tv")
            tpd = C["tv"]
            x1t_tuse[0] = tpd
            for j in range(4):
                di = g * 4 + j
                E("vector", lambda e, di=di, gh=gh, j=j, col=dst_col, dst=dst_sb: e.tensor_copy(
                    dst[:, di * (dst.shape[1] // ND) + col: di * (dst.shape[1] // ND) + col + 128],
                    pt_ps[:, gh * 512 + j * 128: gh * 512 + j * 128 + 128]),
                    wait=[("tv", tpd)] if j == 0 else [], inc="vv")
            bset(bk_, ev=[("vv", C["vv"])])
        return C["vv"]

    wsl_tuse = [0]
    x1_tuse = [0]
    vwr = 0

    for half in range(2):
        for i in range(8):
            tile = half * 8 + i
            extra = []
            if half == 0 and i == 0:
                extra = [("sq", consts_sq), ("gq", consts_gq), ("gv", consts_gv)]
            if x1_tuse[0]:
                extra = extra + [("tv", x1_tuse[0])]
            emit_ln1_tile(xkv[tile * 128:(tile + 1) * 128, :], x1kv, i * 128, extra)
            gamma_pump(6)
        x1_ready = C["vv"]
        if DEBUG and half == 0:
            dma("sync", "sq", dbg_x1[:, :], x1kv[:, :], wait=[("vv", x1_ready)])
            dma("sync", "sq", dbg_st[:, :], st[:, :], wait=[("vv", x1_ready)])

        for jq in range(4):
            for dt_ in range(ND):
                wt = [("tv", wsl_tuse[0])] if (dt_ == 0 and wsl_tuse[0]) else []
                dma("gpsimd", "gq", wsl_kv[:, dt_ * 512:(dt_ + 1) * 512],
                    wk[dt_ * 128:(dt_ + 1) * 128, jq * 512:(jq + 1) * 512], wait=wt)
            wk_ld = C["gq"]
            for jj in range(4):
                j = jq * 4 + jj
                for c in range(2):
                    for dt_ in range(ND):
                        wt = []
                        if dt_ == 0:
                            wt = [("gq", wk_ld), ("vv", x1_ready)] + bwait(7)
                        E("tensor", lambda e, jj=jj, c=c, dt_=dt_: e.matmul(
                            sm_ps[:, 0:512],
                            wsl_kv[:, dt_ * 512 + jj * 128: dt_ * 512 + jj * 128 + 128],
                            x1kv[:, dt_ * 1024 + c * 512: dt_ * 1024 + c * 512 + 512],
                            start=(dt_ == 0), stop=(dt_ == ND - 1)), wait=wt, inc="tv")
                    mmk = C["tv"]
                    wsl_tuse[0] = mmk
                    x1_tuse[0] = mmk
                    E("scalar", lambda e, j=j, c=c, half=half: e.activation(
                        k_sb[:, j * T + half * 1024 + c * 512: j * T + half * 1024 + c * 512 + 512],
                        sm_ps[:, 0:512], ACT.Identity, bias=bk_sb[:, j:j + 1]),
                        wait=[("tv", mmk)], inc="sv")
                    bset(7, ev=[("sv", C["sv"])])
            gamma_pump(2)

        for jc in range(4):
            for dt_ in range(ND):
                wt = [("tv", wsl_tuse[0])] if dt_ == 0 else []
                dma("gpsimd", "gq", wsl_kv[:, dt_ * 512:(dt_ + 1) * 512],
                    wv[dt_ * 128:(dt_ + 1) * 128, jc * 512:(jc + 1) * 512], wait=wt)
            wv_ld = C["gq"]
            for tt in range(8):
                tok = half * 8 + tt
                for dt_ in range(ND):
                    wt = []
                    if dt_ == 0:
                        wt = [("gq", wv_ld)] + bwait(tt)
                    E("tensor", lambda e, tt=tt, dt_=dt_: e.matmul(
                        bank_ap(tt),
                        x1kv[:, dt_ * 1024 + tt * 128: dt_ * 1024 + tt * 128 + 128],
                        wsl_kv[:, dt_ * 512:(dt_ + 1) * 512],
                        start=(dt_ == 0), stop=False), wait=wt, inc="tv")
                E("tensor", lambda e, jc=jc, tt=tt: e.matmul(
                    bank_ap(tt), ones_b[:, :], bv_sb[:, jc * 512:(jc + 1) * 512],
                    start=False, stop=True), inc="tv")
                mmv = C["tv"]
                wsl_tuse[0] = mmv
                x1_tuse[0] = mmv
                wt = [("tv", mmv)]
                if vwr:
                    wt.append(("sq", vwr))
                E("vector", lambda e, tt=tt: e.tensor_copy(vbc[:, :], bank_ap(tt)),
                  wait=wt, inc="vv")
                vev = C["vv"]
                bset(tt, ev=[("vv", vev)])
                for hh in range(4):
                    h = jc * 4 + hh
                    dma("sync", "sq",
                        v_d[h * T + tok * 128: h * T + (tok + 1) * 128, :],
                        vbc[:, hh * 128:(hh + 1) * 128], wait=[("vv", vev)] if hh == 0 else [])
                vwr = C["sq"]
            gamma_pump(2)
    vwr_all = vwr

    for half in range(2):
        for i in range(4):
            tile = half * 4 + i
            extra = [("tv", x1_tuse[0])] if x1_tuse[0] else []
            emit_ln1_tile(xq[tile * 128:(tile + 1) * 128, :], x1q, i * 128, extra)
            gamma_pump(6)
        x1q_ready = C["vv"]
        for jq in range(4):
            for dt_ in range(ND):
                wt = [("tv", wsl_tuse[0])] if dt_ == 0 else []
                dma("gpsimd", "gq", wsl_q[:, dt_ * 512:(dt_ + 1) * 512],
                    wq[dt_ * 128:(dt_ + 1) * 128, jq * 512:(jq + 1) * 512], wait=wt)
            wq_ld = C["gq"]
            for jj in range(4):
                j = jq * 4 + jj
                for dt_ in range(ND):
                    wt = []
                    if dt_ == 0:
                        wt = [("gq", wq_ld), ("vv", x1q_ready)] + bwait(7)
                    E("tensor", lambda e, jj=jj, dt_=dt_: e.matmul(
                        sm_ps[:, 0:512],
                        wsl_q[:, dt_ * 512 + jj * 128: dt_ * 512 + jj * 128 + 128],
                        x1q[:, dt_ * 512: dt_ * 512 + 512],
                        start=(dt_ == 0), stop=(dt_ == ND - 1)), wait=wt, inc="tv")
                mmq = C["tv"]
                wsl_tuse[0] = mmq
                x1_tuse[0] = mmq
                E("scalar", lambda e, j=j, half=half: e.activation(
                    q_sb[:, j * TOWN + half * 512: j * TOWN + half * 512 + 512],
                    sm_ps[:, 0:512], ACT.Identity, bias=bq_sb[:, j:j + 1]),
                    wait=[("tv", mmq)], inc="sv")
                bset(7, ev=[("sv", C["sv"])])
            gamma_pump(4)

    gamma_pump(256)
    for c in range(4):
        EV(lambda e, c=c: e.reduce_sum(st[:, 10 + c:11 + c], gacc[:, c * 32:(c + 1) * 32], AXX),
           wait=[("vv", gamma_use[127])] if c == 0 else [])
    for c in range(4):
        EV(lambda e, c=c: e.reduce_sum(st[:, 14 + c:15 + c],
                                       gacc[:, 128 + c * 32:128 + (c + 1) * 32], AXX),
           wait=[("vv", gamma_use[255])] if c == 0 else [])
    vg = C["vv"]
    gp128 = gpart_d[0:1, 0:1024].rearrange("o (p c) -> (o p) c", p=128)
    dma("sync", "sq", gp128, st[:, 10:18], wait=[("vv", vg)])
    gp_wr = C["sq"]
    dma("sync", "sq", g_b[0:1, 0:1024], gpart_d[0:1, 0:1024], wait=[("sq", gp_wr)])
    kst_ld = C["sq"]
    kS = st[0:1, 16:24]
    kC = st[0:1, 24:32]
    kY = st[0:1, 40:48]
    kT = st[0:1, 48:56]
    EV(lambda e: e.tensor_copy(kS, g_b[0:1, 0:8]), wait=[("sq", kst_ld)])
    EV(lambda e: e.tensor_scalar(kC, kS, 0.0, None, ALU.mult))
    for i in range(1, 128):
        xi = g_b[0:1, i * 8:(i + 1) * 8]
        EV(lambda e, xi=xi: e.tensor_tensor(kY, xi, kC, ALU.subtract))
        EV(lambda e: e.tensor_tensor(kT, kS, kY, ALU.add))
        EV(lambda e: e.tensor_tensor(kC, kT, kS, ALU.subtract))
        EV(lambda e: e.tensor_tensor(kC, kC, kY, ALU.subtract))
        EV(lambda e: e.tensor_copy(kS, kT))
    EV(lambda e: e.tensor_scalar(kC, kC, -1.0, None, ALU.mult))
    fS = st[0:1, 56:57]
    fC = st[0:1, 57:58]
    fY = st[0:1, 58:59]
    fT = st[0:1, 59:60]
    for g in range(2):
        lanes = ([st[0:1, 16 + g * 4 + k:17 + g * 4 + k] for k in range(4)] +
                 [st[0:1, 24 + g * 4 + k:25 + g * 4 + k] for k in range(4)])
        EV(lambda e, l0=lanes[0]: e.tensor_copy(fS, l0))
        EV(lambda e: e.tensor_scalar(fC, fS, 0.0, None, ALU.mult))
        for x in lanes[1:]:
            EV(lambda e, x=x: e.tensor_tensor(fY, x, fC, ALU.subtract))
            EV(lambda e: e.tensor_tensor(fT, fS, fY, ALU.add))
            EV(lambda e: e.tensor_tensor(fC, fT, fS, ALU.subtract))
            EV(lambda e: e.tensor_tensor(fC, fC, fY, ALU.subtract))
            EV(lambda e: e.tensor_copy(fS, fT))
        EV(lambda e: e.tensor_tensor(fT, fS, fC, ALU.subtract))
        EV(lambda e, g=g: e.tensor_scalar(gval[:, g:g + 1], fT, 1.0 / NW, None, ALU.mult))
    EV(lambda e: e.tensor_scalar(st[0:1, 62:63], gval[:, 0:1], 0.0, 0.5,
                                 ALU.mult, ALU.add))
    EV(lambda e: e.tensor_scalar(st[0:1, 63:64], st[0:1, 62:63], 2.0 ** -27,
                                 None, ALU.add))
    EV(lambda e: e.tensor_tensor(st[0:1, 63:64], st[0:1, 63:64], st[0:1, 62:63],
                                 ALU.subtract))
    EV(lambda e: e.tensor_scalar(st[0:1, 63:64], st[0:1, 63:64], -(2.0 ** 27), 1.0,
                                 ALU.mult, ALU.add))
    for g in range(2):
        EV(lambda e, g=g: e.tensor_scalar(st[0:1, 60:61], gval[:, g:g + 1],
                                          EPS_BL, 0.5, ALU.add, ALU.mult))
        EV(lambda e, g=g: e.tensor_scalar(st[0:1, 61:62], gval[:, g:g + 1],
                                          EPS_BL, 2.0 ** -25, ALU.add, ALU.mult))
        EV(lambda e: e.tensor_tensor(st[0:1, 61:62], st[0:1, 61:62], st[0:1, 63:64],
                                     ALU.mult))
        EV(lambda e, g=g: e.tensor_tensor(gval[:, 2 + g:3 + g], st[0:1, 60:61],
                                          st[0:1, 61:62], ALU.add))
        EV(lambda e, g=g: e.tensor_tensor(st[0:1, 62:63], gval[:, 2 + g:3 + g],
                                          st[0:1, 60:61], ALU.subtract))
        EV(lambda e, g=g: e.tensor_tensor(gval[:, 4 + g:5 + g], st[0:1, 61:62],
                                          st[0:1, 62:63], ALU.subtract))
        EV(lambda e, g=g: e.tensor_scalar(gval[:, 6 + g:7 + g], gval[:, 4 + g:5 + g],
                                          -1.0, None, ALU.mult))
    vgs = C["vv"]
    for i, sc in enumerate([2, 3, 0, 1, 4, 5, 6, 7]):
        E("tensor", lambda e, i=i, sc=sc: e.matmul(
            sm_ps[:, 2 + i:3 + i], ones_f[:, :], gval[:, sc:sc + 1], start=True, stop=True),
            wait=([("vv", vgs), ("gv", consts_gv)] + bwait(7) if i == 0 else []), inc="tv")
    tbc = C["tv"]
    E("vector", lambda e: e.tensor_copy(thr[:, 0:8], sm_ps[:, 2:10]), wait=[("tv", tbc)], inc="vv")
    thr_done = C["vv"]
    bset(7, ev=[("vv", thr_done)])

    tibuf = [tin_a, tin_b]
    tobuf = [to1, to2]
    tern_out = []
    tern_v = []
    tern_n = [0]

    def tern_pump(kk):
        for _ in range(kk):
            i = tern_n[0]
            if i >= 256:
                return
            tern_n[0] = i + 1
            if i < 128:
                s = i
                src = wfcT[(s // 8) * 128:(s // 8 + 1) * 128, (s % 8) * 1024:(s % 8 + 1) * 1024]
                dst = tfc_d[(s // 8) * 128:(s // 8 + 1) * 128, (s % 8) * 1024:(s % 8 + 1) * 1024]
                tc = 0
            else:
                s = i - 128
                src = wprojT[(s // 2) * 128:(s // 2 + 1) * 128, (s % 2) * 1024:(s % 2 + 1) * 1024]
                dst = tpj_d[(s // 2) * 128:(s // 2 + 1) * 128, (s % 2) * 1024:(s % 2 + 1) * 1024]
                tc = 1
            ib = tibuf[i % 2]
            ob = tobuf[i % 2]
            w = [("vv", tern_v[i - 2])] if i >= 2 else [("vv", thr_done)]
            w.append(("cq", C["cq"]))
            dma("scalar", "cq", ib[:, :], src, wait=w)
            ld = C["cq"]
            E("vector", lambda e, b=ib, tc=tc: e.tensor_scalar(
                g_a[:, :], b[:, :], thr[:, tc:tc + 1], None, ALU.subtract),
                wait=[("cq", ld), ("vv", thr_done)], inc="vv")
            E("vector", lambda e, tc=tc: e.tensor_scalar(
                tm1[:, :], g_a[:, :], thr[:, 4 + tc:5 + tc], None, ALU.is_gt), inc="vv")
            E("vector", lambda e, b=ib, tc=tc: e.tensor_scalar(
                g_a[:, :], b[:, :], thr[:, tc:tc + 1], None, ALU.add), inc="vv")
            E("vector", lambda e, tc=tc: e.tensor_scalar(
                tm0[:, :], g_a[:, :], thr[:, 6 + tc:7 + tc], None, ALU.is_lt), inc="vv")
            wst = [("gq", tern_out[i - 2])] if i >= 2 else []
            E("vector", lambda e, ob=ob: e.tensor_tensor(
                ob[:, :], tm1[:, :], tm0[:, :], ALU.subtract), wait=wst, inc="vv")
            tern_v.append(C["vv"])
            dma("gpsimd", "gq", dst, ob[:, :],
                wait=[("vv", tern_v[-1]), ("gq", C["gq"])])
            tern_out.append(C["gq"])

    prev_exp = [("sv", C["sv"])]
    prev_pt = [0]
    prev_avev = [0]
    vstr_use = [0, 0]
    vld_prev = [0, 0]
    pts_use = [0, 0]

    for u in range(8):
        width = (2 * u + 2) * 128
        nch = (width + 511) // 512
        nkt = 2 * u + 2
        for h in range(H):
            idx = u * H + h
            vb = [vs_a, vs_b][idx % 2]
            wt = [("sq", vwr_all)] if idx == 0 else []
            if vstr_use[idx % 2]:
                wt.append(("tv", vstr_use[idx % 2]))
            dma("sync", "sq",
                vb[:, 0:width].rearrange("p (n c) -> p n c", c=128),
                v_d[h * T: h * T + width, :].rearrange("(n p) c -> p n c", p=128),
                wait=wt)
            vld = C["sq"]
            vld_prev[idx % 2] = vld

            for c in range(nch):
                w0 = c * 512
                cw = min(512, width - w0)
                wt = bwait(c) + ([("vv", thr_done)] if idx == 0 else [])
                E("tensor", lambda e, u=u, h=h, w0=w0, cw=cw: e.matmul(
                    s_ps[:, w0:w0 + cw],
                    q_sb[:, h * TOWN + u * 128: h * TOWN + u * 128 + 128],
                    k_sb[:, h * T + w0: h * T + w0 + cw],
                    start=True, stop=True), wait=wt, inc="tv")
            smm = C["tv"]
            E("vector", lambda e, u=u: e.tensor_tensor(
                s_ps[:, 2 * u * 128: 2 * u * 128 + 128],
                s_ps[:, 2 * u * 128: 2 * u * 128 + 128], mska[:, :], ALU.add),
                wait=[("tv", smm)], inc="vv")
            E("vector", lambda e, u=u: e.tensor_tensor(
                s_ps[:, (2 * u + 1) * 128: (2 * u + 1) * 128 + 128],
                s_ps[:, (2 * u + 1) * 128: (2 * u + 1) * 128 + 128], mskb[:, :], ALU.add),
                inc="vv")
            E("vector", lambda e, width=width: e.reduce_max(
                expb[:, 0:1], s_ps[:, 0:width], AXX, negate=True), inc="vv")
            EV(lambda e: e.tensor_scalar(expb[:, 1:2], expb[:, 0:1], SM_SCALE, None, ALU.mult))
            vmax = C["vv"]
            for c in range(nch):
                w0 = c * 512
                cw = min(512, width - w0)
                wt = []
                if c == 0:
                    wt = [("vv", vmax)]
                    if prev_pt[0]:
                        wt.append(("tv", prev_pt[0]))
                E("scalar", lambda e, w0=w0, cw=cw, c=c: e.activation(
                    p_sb[:, w0:w0 + cw], s_ps[:, w0:w0 + cw], ACT.Exp,
                    bias=expb[:, 1:2], scale=SM_SCALE, accum_out=lsum[:, c:c + 1]),
                    wait=wt, inc="sv")
            sexp = C["sv"]
            bset(*range(nch), ev=[("sv", sexp), ("vv", vmax)])
            EV(lambda e, nch=nch: e.reduce_sum(st[:, 12:13], lsum[:, 0:nch], AXX),
               wait=[("sv", sexp)])
            EV(lambda e: e.reciprocal(st[:, 13:14], st[:, 12:13]))
            EV(lambda e, width=width: e.tensor_scalar(
                p_sb[:, 0:width], p_sb[:, 0:width], st[:, 13:14], None, ALU.mult))
            vnorm = C["vv"]
            for g in range((nkt + 3) // 4):
                gh = g % 2
                bk_ = 4 + gh
                blocks = list(range(g * 4, min(g * 4 + 4, nkt)))
                for bi, kt in enumerate(blocks):
                    wt = []
                    if bi == 0:
                        wt = [("vv", vnorm)] + bwait(bk_)
                    E("tensor", lambda e, kt=kt, gh=gh, bi=bi: e.matmul(
                        pt_ps[:, gh * 512 + bi * 128: gh * 512 + bi * 128 + 128],
                        p_sb[:, kt * 128:(kt + 1) * 128], ident[:, :],
                        start=True, stop=True), wait=wt, inc="tv")
                tptg = C["tv"]
                prev_pt[0] = tptg
                wt = [("tv", tptg)]
                if pts_use[gh]:
                    wt.append(("tv", pts_use[gh]))
                E("vector", lambda e, gh=gh, nb=len(blocks): e.tensor_copy(
                    pt_sb[:, gh * 512: gh * 512 + nb * 128],
                    pt_ps[:, gh * 512: gh * 512 + nb * 128]), wait=wt, inc="vv")
                bset(bk_, ev=[("vv", C["vv"])])
                vevg = C["vv"]
                for bi, kt in enumerate(blocks):
                    wt = []
                    if bi == 0:
                        wt = [("vv", vevg), ("sq", vld)]
                        if kt == 0:
                            wt = wt + bwait(6)
                    E("tensor", lambda e, kt=kt, gh=gh, bi=bi, vb=vb,
                        first=(kt == 0), last=(kt == nkt - 1): e.matmul(
                        av_ps[:, 0:128],
                        vb[:, kt * 128:(kt + 1) * 128],
                        pt_sb[:, gh * 512 + bi * 128: gh * 512 + bi * 128 + 128],
                        start=first, stop=last), wait=wt, inc="tv")
                pts_use[gh] = C["tv"]
            tav = C["tv"]
            vstr_use[idx % 2] = tav
            E("vector", lambda e, h=h, u=u: e.tensor_copy(
                attn_sb[:, h * TOWN + u * 128: h * TOWN + u * 128 + 128], av_ps[:, 0:128]),
                wait=[("tv", tav)], inc="vv")
            bset(6, ev=[("vv", C["vv"])])
            tern_pump(2)

    if DEBUG:
        dma("sync", "sq", dbg_k[:, :], k_sb[:, :], wait=[("sv", C["sv"])])
        dma("sync", "sq", dbg_q[:, :], q_sb[:, :], wait=[("sv", C["sv"])])
        dma("sync", "sq", dbg_attn[:, :], attn_sb[:, :], wait=[("vv", C["vv"])])
    tern_pump(256)
    tern_all_gq = C["gq"]
    attn_done_v = C["vv"]
    attn_done_s = C["sv"]
    attn_done_t = C["tv"]

    op_ev = [0] * 8
    wch_tuse = [0, 0]
    xq_vuse = [0]
    for ec in range(4):
        for f in range(ND):
            i = ec * ND + f
            buf = wch_a if i % 2 == 0 else wch_b
            wt = [("tv", wch_tuse[i % 2])] if wch_tuse[i % 2] else [("vv", attn_done_v)]
            dma("gpsimd", "gq", buf[:, :],
                wo[f * 128:(f + 1) * 128, ec * 512:(ec + 1) * 512], wait=wt)
            wo_ld = C["gq"]
            for t in range(8):
                wt = []
                if t == 0:
                    wt = [("gq", wo_ld)]
                if f == 0:
                    wt = wt + bwait(t) + [("vv", attn_done_v)]
                E("tensor", lambda e, f=f, t=t, buf=buf: e.matmul(
                    bank_ap(t),
                    attn_sb[:, f * TOWN + t * 128: f * TOWN + t * 128 + 128],
                    buf[:, :], start=(f == 0), stop=False), wait=wt, inc="tv")
            wch_tuse[i % 2] = C["tv"]
        for t in range(8):
            E("tensor", lambda e, ec=ec, t=t: e.matmul(
                bank_ap(t), ones_b[:, :], bo_sb[:, ec * 512:(ec + 1) * 512],
                start=False, stop=True), inc="tv")
            topt = C["tv"]
            wt = [("vv", xq_vuse[0])] if xq_vuse[0] else []
            dma("sync", "sq", xq_c[:, :],
                xq[t * 128:(t + 1) * 128, ec * 512:(ec + 1) * 512], wait=wt)
            xq_ld = C["sq"]
            E("vector", lambda e, t=t, ec=ec: e.tensor_tensor(
                xnew_sb[:, t * D + ec * 512: t * D + ec * 512 + 512],
                bank_ap(t), xq_c[:, :], ALU.add),
                wait=[("tv", topt), ("sq", xq_ld)], inc="vv")
            op_ev[t] = C["vv"]
            xq_vuse[0] = C["vv"]
            bset(t, ev=[("vv", C["vv"])])

    xnew_wr = [0] * 8
    xhat_dma = [0]
    prev_tv = [C["vv"]]
    for t in range(8):
        dma("sync", "sq", xnew_d[t * 128:(t + 1) * 128, :],
            xnew_sb[:, t * D: t * D + D], wait=[("vv", op_ev[t])])
        xnew_wr[t] = C["sq"]
        E("vector", lambda e, t=t: e.reduce_sum(st[:, 16:17], xnew_sb[:, t * D: t * D + D], AXX),
          wait=[("vv", op_ev[t])], inc="vv")
        for cc in range(4):
            E("scalar", lambda e, t=t, cc=cc: e.activation(
                junk[:, :], xnew_sb[:, t * D + cc * 512: t * D + (cc + 1) * 512], ACT.Square,
                accum_out=st[:, 1 + cc:2 + cc]),
                wait=[("vv", prev_tv[0])] if cc == 0 else [], inc="sv")
        s0 = C["sv"]
        v0 = C["vv"]
        EV(lambda e: e.reduce_sum(st[:, 17:18], st[:, 1:5], AXX), wait=[("sv", s0)])
        EV(lambda e: e.tensor_scalar(st[:, 18:19], st[:, 16:17], 1.0 / D, None, ALU.mult),
           wait=[("vv", v0)])
        EV(lambda e: e.tensor_scalar(st[:, 19:20], st[:, 17:18], 1.0 / D, None, ALU.mult))
        EV(lambda e: e.tensor_tensor(st[:, 20:21], st[:, 18:19], st[:, 18:19], ALU.mult))
        EV(lambda e: e.scalar_tensor_tensor(st[:, 21:22], st[:, 19:20], EPS_LN,
                                            st[:, 20:21], ALU.add, ALU.subtract))
        vz2 = C["vv"]
        ES(lambda e: e.activation(st[:, 22:23], st[:, 21:22], ACT.Ln), wait=[("vv", vz2)])
        ES(lambda e: e.activation(st[:, 23:24], st[:, 22:23], ACT.Exp, scale=-0.5))
        sr2 = C["sv"]
        EV(lambda e, t=t: e.tensor_scalar(w2[:, :], xnew_sb[:, t * D: t * D + D],
                                          st[:, 18:19], None, ALU.subtract), wait=[("vv", vz2)])
        vw2 = C["vv"]
        EV(lambda e: e.reduce_sum(st[:, 24:25], w2[:, :], AXX))
        for cc in range(4):
            E("scalar", lambda e, cc=cc: e.activation(
                junk[:, :], w2[:, cc * 512:(cc + 1) * 512], ACT.Square,
                accum_out=st[:, 1 + cc:2 + cc]),
                wait=[("vv", vw2), ("sv", sr2)] if cc == 0 else [], inc="sv")
        sw2 = C["sv"]
        EV(lambda e: e.reduce_sum(st[:, 25:26], st[:, 1:5], AXX), wait=[("sv", sw2)])
        EV(lambda e: e.tensor_scalar(st[:, 26:27], st[:, 24:25], 1.0 / D, None, ALU.mult))
        EV(lambda e: e.tensor_scalar(w2[:, :], w2[:, :], st[:, 26:27], None, ALU.subtract))
        EV(lambda e: e.reduce_max(st[:, 27:28], w2[:, :], AXX, apply_absolute_value=True))
        vA = C["vv"]
        EV(lambda e: e.tensor_scalar(st[:, 28:29], st[:, 25:26], 1.0 / D, None, ALU.mult))
        EV(lambda e: e.tensor_tensor(st[:, 29:30], st[:, 26:27], st[:, 26:27], ALU.mult))
        EV(lambda e: e.tensor_tensor(st[:, 30:31], st[:, 28:29], st[:, 29:30], ALU.subtract))
        EV(lambda e: e.tensor_tensor(st[:, 31:32], st[:, 23:24], st[:, 23:24], ALU.mult))
        EV(lambda e: e.tensor_tensor(st[:, 32:33], st[:, 30:31], st[:, 31:32], ALU.mult))
        EV(lambda e: e.tensor_scalar(st[:, 33:34], st[:, 32:33], EPS_BL, None, ALU.add))
        vz3 = C["vv"]
        ES(lambda e: e.activation(st[:, 34:35], st[:, 33:34], ACT.Ln), wait=[("vv", vz3)])
        ES(lambda e: e.activation(st[:, 35:36], st[:, 34:35], ACT.Exp, scale=-0.5))
        sr3 = C["sv"]
        EV(lambda e: e.tensor_tensor(st[:, 36:37], st[:, 23:24], st[:, 27:28], ALU.mult),
           wait=[("sv", sr3)])
        EV(lambda e: e.tensor_scalar(st[:, 37:38], st[:, 36:37], st[:, 35:36], 1.0 / QB,
                                     ALU.mult, ALU.mult))
        EV(lambda e, t=t: e.tensor_tensor(sfc[:, t:t + 1], st[:, 37:38], thr[:, 2:3], ALU.mult))
        EV(lambda e: e.reciprocal(st[:, 39:40], st[:, 27:28]))
        EV(lambda e: e.tensor_scalar(st[:, 39:40], st[:, 39:40], QB, None, ALU.mult))
        wt = [("sq", xhat_dma[0])] if xhat_dma[0] else []
        EV(lambda e: e.tensor_scalar(xhat_t[:, :], w2[:, :], st[:, 39:40], None, ALU.mult),
           wait=wt)
        EV(lambda e: e.tensor_scalar(xhat_t[:, :], xhat_t[:, :], CLIP_HI, CLIP_LO,
                                     ALU.min, ALU.max))
        vxh = C["vv"]
        prev_tv[0] = vxh
        dma("sync", "sq", xhat_d[t * 128:(t + 1) * 128, :], xhat_t[:, :], wait=[("vv", vxh)])
        xhat_dma[0] = C["sq"]

    xhat_wr = C["sq"]
    sfc_done = C["vv"]
    for dt_ in range(ND):
        wt = [("sq", xhat_wr), ("tv", attn_done_t)] if dt_ == 0 else []
        dmat(q_sb[:, dt_ * 1024:(dt_ + 1) * 1024],
             xhat_d[:, dt_ * 128:(dt_ + 1) * 128], wait=wt)
    xfm_ld = C["sq"]

    fc_ev = [0] * 8
    fc_wuse = [0, 0]
    for oc in range(16):
        for dt_ in range(ND):
            i = oc * ND + dt_
            buf = wch_a if i % 2 == 0 else wch_b
            wt = [("tv", fc_wuse[i % 2])] if fc_wuse[i % 2] else []
            if i < 2:
                wt = wt + [("sq", xfm_ld), ("gq", tern_all_gq)]
            dma("sync", "sq", buf[:, :],
                tfc_d[dt_ * 128:(dt_ + 1) * 128, oc * 512:(oc + 1) * 512], wait=wt)
            tld = C["sq"]
            for t in range(8):
                wt = []
                if t == 0:
                    wt = [("sq", tld)]
                if dt_ == 0:
                    wt = wt + (bwait(t) if fc_ev[t] else bwait(t) + [("vv", sfc_done)])
                E("tensor", lambda e, dt_=dt_, t=t, buf=buf: e.matmul(
                    bank_ap(t),
                    q_sb[:, dt_ * 1024 + t * 128: dt_ * 1024 + t * 128 + 128],
                    buf[:, :], start=(dt_ == 0), stop=(dt_ == ND - 1)), wait=wt, inc="tv")
            fc_wuse[i % 2] = C["tv"]
        tfc_mm = C["tv"]
        for t in range(8):
            E("scalar", lambda e, t=t, oc=oc: e.activation(
                h_ap(t, oc * 512, 512), bank_ap(t), ACT.Gelu, scale=sfc[:, t:t + 1],
                accum_out=gacc[:, t * 16 + oc: t * 16 + oc + 1]),
                wait=[("tv", tfc_mm)], inc="sv")
            fc_ev[t] = C["sv"]
            bset(t, ev=[("sv", C["sv"])])

    hhat_dma = [0]
    prev_svq = [C["sv"]]
    for t in range(8):
        EV(lambda e, t=t: e.reduce_sum(st[:, 16:17], gacc[:, t * 16:(t + 1) * 16], AXX),
           wait=[("sv", fc_ev[t])])
        EV(lambda e: e.tensor_scalar(st[:, 18:19], st[:, 16:17], 1.0 / DFF, None, ALU.mult))
        vm = C["vv"]
        for oc in range(16):
            wt = []
            if oc == 0:
                wt = [("sv", fc_ev[t]), ("vv", prev_svq[0])]
            E("scalar", lambda e, t=t, oc=oc: e.activation(
                junk[:, :], h_ap(t, oc * 512, 512), ACT.Square,
                accum_out=st[:, 40 + oc:41 + oc]), wait=wt, inc="sv")
        ssq = C["sv"]
        EV(lambda e: e.reduce_sum(st[:, 17:18], st[:, 40:56], AXX), wait=[("sv", ssq)])
        EV(lambda e: e.tensor_scalar(st[:, 19:20], st[:, 17:18], 1.0 / DFF, None, ALU.mult))
        EV(lambda e: e.tensor_tensor(st[:, 20:21], st[:, 18:19], st[:, 18:19], ALU.mult))
        EV(lambda e: e.tensor_tensor(st[:, 21:22], st[:, 19:20], st[:, 20:21], ALU.subtract))
        EV(lambda e: e.tensor_scalar(st[:, 21:22], st[:, 21:22], EPS_BL, None, ALU.add))
        vz = C["vv"]
        for oc in range(16):
            EV(lambda e, t=t, oc=oc: e.tensor_scalar(
                uq[:, 0:512], h_ap(t, oc * 512, 512), st[:, 18:19], None, ALU.subtract),
                wait=[("vv", vm)] if oc == 0 else [])
            EV(lambda e, oc=oc: e.reduce_max(st[:, 40 + oc:41 + oc], uq[:, 0:512], AXX,
                                             apply_absolute_value=True))
        EV(lambda e: e.reduce_max(st[:, 27:28], st[:, 40:56], AXX))
        vA = C["vv"]
        ES(lambda e: e.activation(st[:, 34:35], st[:, 21:22], ACT.Ln), wait=[("vv", vz)])
        ES(lambda e: e.activation(st[:, 35:36], st[:, 34:35], ACT.Exp, scale=-0.5))
        sr = C["sv"]
        prev_svq[0] = C["vv"]
        EV(lambda e: e.tensor_scalar(st[:, 37:38], st[:, 27:28], 1.0 / QB, st[:, 35:36],
                                     ALU.mult, ALU.mult), wait=[("sv", sr), ("vv", vA)])
        EV(lambda e, t=t: e.tensor_tensor(spj[:, t:t + 1], st[:, 37:38], thr[:, 3:4], ALU.mult))
        EV(lambda e: e.reciprocal(st[:, 39:40], st[:, 27:28]), wait=[("vv", vA)])
        EV(lambda e: e.tensor_scalar(st[:, 39:40], st[:, 39:40], QB, None, ALU.mult))
        hq = 0
        for oc in range(16):
            wt = []
            if oc == 0 and hhat_dma[0]:
                wt.append(("sq", hhat_dma[0]))
            EV(lambda e, t=t, oc=oc: e.tensor_scalar(
                uq[:, 0:512], h_ap(t, oc * 512, 512), st[:, 18:19], None, ALU.subtract),
                wait=wt)
            EV(lambda e: e.tensor_scalar(uq[:, 512:1024], uq[:, 0:512], st[:, 39:40],
                                         None, ALU.mult))
            EV(lambda e, t=t, oc=oc: e.tensor_scalar(
                h_ap(t, oc * 512, 512), uq[:, 512:1024], CLIP_HI, CLIP_LO, ALU.min, ALU.max))
            hq = C["vv"]
        dma("sync", "sq", hhat_d[t * 128:(t + 1) * 128, :], h_ap(t, 0, DFF), wait=[("vv", hq)])
        hhat_dma[0] = C["sq"]
    hhat_wr = C["sq"]
    spj_done = C["vv"]

    for o in range(NO):
        wt = [("sq", hhat_wr)] if o == 0 else []
        dmat(hfm_ap(o, 0, 1024), hhat_d[:, o * 128:(o + 1) * 128], wait=wt)
    hfm_ld = C["sq"]

    pj_ev = [0] * 8
    pj_wuse = [0, 0]
    xn_vuse = [0]
    out_duse = [0]
    for ec in range(4):
        for o in range(NO):
            i = ec * NO + o
            buf = wch_a if i % 2 == 0 else wch_b
            wt = [("tv", pj_wuse[i % 2])] if pj_wuse[i % 2] else [("sq", hfm_ld)]
            dma("sync", "sq", buf[:, :],
                tpj_d[o * 128:(o + 1) * 128, ec * 512:(ec + 1) * 512], wait=wt)
            tld = C["sq"]
            for t in range(8):
                wt = []
                if t == 0:
                    wt = [("sq", tld)]
                if o == 0:
                    wt = wt + bwait(t) + ([("sq", hfm_ld)] if ec == 0 else [])
                E("tensor", lambda e, o=o, t=t, buf=buf: e.matmul(
                    bank_ap(t), hfm_ap(o, t * 128, 128),
                    buf[:, :], start=(o == 0), stop=(o == NO - 1)), wait=wt, inc="tv")
            pj_wuse[i % 2] = C["tv"]
        tpj_mm = C["tv"]
        for t in range(8):
            wt = [("sq", xnew_wr[t])]
            if xn_vuse[0]:
                wt.append(("vv", xn_vuse[0]))
            dma("sync", "sq", xq_c[:, :],
                xnew_d[t * 128:(t + 1) * 128, ec * 512:(ec + 1) * 512], wait=wt)
            xld = C["sq"]
            wt = [("tv", tpj_mm), ("sq", xld)]
            if t == 0 and ec == 0:
                wt.append(("vv", spj_done))
            if out_duse[0]:
                wt.append(("sq", out_duse[0]))
            E("vector", lambda e, t=t: e.scalar_tensor_tensor(
                out_c[:, :], bank_ap(t), spj[:, t:t + 1], xq_c[:, :], ALU.mult, ALU.add),
                wait=wt, inc="vv")
            pj_ev[t] = C["vv"]
            xn_vuse[0] = C["vv"]
            bset(t, ev=[("vv", C["vv"])])
            dma("sync", "sq", out[t * 128:(t + 1) * 128, ec * 512:(ec + 1) * 512], out_c[:, :],
                wait=[("vv", pj_ev[t])])
            out_duse[0] = C["sq"]

    final_sq = C["sq"]

    with nc.Block() as blk:
        def mk(eng_name):
            def body(e):
                for waits, fn, incs, n in OPS[eng_name]:
                    for (s, v) in waits:
                        if v > 0:
                            e.wait_ge(SEM[s], v)
                    ins = fn(e)
                    if incs is not None:
                        ins.then_inc(SEM[incs], n)
                if eng_name == "sync":
                    e.wait_ge(SEM["sq"], final_sq)
            return body

        blk.sync(mk("sync"))
        blk.gpsimd(mk("gpsimd"))
        blk.tensor(mk("tensor"))
        blk.vector(mk("vector"))
        blk.scalar(mk("scalar"))

    for cm in reversed(mgr):
        cm.__exit__(None, None, None)
    return nc


_CACHE = {}


def kernel(L0_attn_query, L0_attn_w_q, L0_attn_b_q, L0_attn_w_k, L0_attn_b_k,
           L0_attn_w_v, L0_attn_b_v, L0_attn_w_o, L0_attn_b_o, L0_ffn_w_fc, L0_ffn_w_proj):
    x = np.asarray(L0_attn_query, dtype=np.float32)
    wq_ = np.ascontiguousarray(np.asarray(L0_attn_w_q, np.float32))
    wk_ = np.ascontiguousarray(np.asarray(L0_attn_w_k, np.float32))
    wv_ = np.ascontiguousarray(np.asarray(L0_attn_w_v, np.float32))
    wo_ = np.ascontiguousarray(np.asarray(L0_attn_w_o, np.float32))
    bq_ = np.ascontiguousarray(np.asarray(L0_attn_b_q, np.float32).reshape(16, 128).T)
    bk_ = np.ascontiguousarray(np.asarray(L0_attn_b_k, np.float32).reshape(16, 128).T)
    bv_ = np.ascontiguousarray(np.asarray(L0_attn_b_v, np.float32).reshape(1, D))
    bo_ = np.ascontiguousarray(np.asarray(L0_attn_b_o, np.float32).reshape(1, D))
    wfcT_ = np.ascontiguousarray(np.asarray(L0_ffn_w_fc, np.float32).T)
    wprojT_ = np.ascontiguousarray(np.asarray(L0_ffn_w_proj, np.float32).T)
    ident_ = np.eye(128, dtype=ml_dtypes.bfloat16)

    tri = np.triu(np.full((128, 128), -1e10, np.float32), k=1)
    mask_sets = {
        0: (np.zeros((128, 128), np.float32), tri),
        1: (tri, np.full((128, 128), -1e10, np.float32)),
    }

    if "nc" not in _CACHE:
        _CACHE["nc"] = build_program()
    nc = _CACHE["nc"]

    in_maps = []
    row_sets = []
    for c in range(8):
        b, p = c // 2, c % 2
        qtiles = [2 * u + 1 - p for u in range(8)]
        rows = np.concatenate([np.arange(qt * 128, qt * 128 + 128) for qt in qtiles])
        row_sets.append((b, rows))
        ma, mb = mask_sets[p]
        in_maps.append({
            "xkv": np.ascontiguousarray(x[b]),
            "xq": np.ascontiguousarray(x[b][rows]),
            "wq": wq_, "wk": wk_, "wv": wv_, "wo": wo_,
            "bq": bq_, "bk": bk_, "bv": bv_, "bo": bo_,
            "wfcT": wfcT_, "wprojT": wprojT_,
            "maska": ma, "maskb": mb, "ident": ident_,
        })

    res = run_bass_kernel_spmd(nc, in_maps, core_ids=list(range(8)))
    outp = np.empty((B, T, D), np.float32)
    for c in range(8):
        b, rows = row_sets[c]
        outp[b][rows] = res.results[c]["out"]
    return outp


# version 11 (final)
# speedup vs baseline: 1.0121x; 1.0005x over previous
import math
import numpy as np
import ml_dtypes
import concourse.bass as bass
import concourse.mybir as mybir
from concourse.bass_utils import run_bass_kernel_spmd

F32 = mybir.dt.float32
BF16 = mybir.dt.bfloat16
U8 = mybir.dt.uint8
ALU = mybir.AluOpType
ACT = mybir.ActivationFunctionType
AXX = mybir.AxisListType.X

B, T, D = 4, 2048, 2048
H, DH = 16, 128
DFF = 8192
NT = 16
NOWN = 8
TOWN = NOWN * 128
ND = D // 128
NO = DFF // 128

SM_SCALE = 1.0 / math.sqrt(128.0)
EPS_LN = 1e-5
EPS_BL = 1e-6
QB = 128.0
CLIP_HI = 128.0 - 1e-6
CLIP_LO = -128.0 + 1e-6
NW = float(DFF * D)

ARENA = 212832


DEBUG = False


def build_program():
    nc = bass.Bass("TRN2", target_bir_lowering=False)
    ikind = "ExternalOutput" if DEBUG else "Internal"

    xkv = nc.dram_tensor("xkv", [T, D], F32, kind="ExternalInput")
    xq = nc.dram_tensor("xq", [TOWN, D], F32, kind="ExternalInput")
    wq = nc.dram_tensor("wq", [D, D], F32, kind="ExternalInput")
    wk = nc.dram_tensor("wk", [D, D], F32, kind="ExternalInput")
    wv = nc.dram_tensor("wv", [D, D], F32, kind="ExternalInput")
    wo = nc.dram_tensor("wo", [D, D], F32, kind="ExternalInput")
    bq = nc.dram_tensor("bq", [128, ND], F32, kind="ExternalInput")
    bk = nc.dram_tensor("bk", [128, ND], F32, kind="ExternalInput")
    bv = nc.dram_tensor("bv", [1, D], F32, kind="ExternalInput")
    bo = nc.dram_tensor("bo", [1, D], F32, kind="ExternalInput")
    wfcT = nc.dram_tensor("wfcT", [D, DFF], F32, kind="ExternalInput")
    wprojT = nc.dram_tensor("wprojT", [DFF, D], F32, kind="ExternalInput")
    maska = nc.dram_tensor("maska", [128, 128], F32, kind="ExternalInput")
    maskb = nc.dram_tensor("maskb", [128, 128], F32, kind="ExternalInput")
    ident_in = nc.dram_tensor("ident", [128, 128], BF16, kind="ExternalInput")
    out = nc.dram_tensor("out", [TOWN, D], F32, kind="ExternalOutput")

    tfc_d = nc.dram_tensor("tfc_d", [D, DFF], BF16, kind=ikind)
    tpj_d = nc.dram_tensor("tpj_d", [DFF, D], BF16, kind=ikind)
    v_d = nc.dram_tensor("v_d", [H * T, DH], BF16, kind=ikind)
    xnew_d = nc.dram_tensor("xnew_d", [TOWN, D], F32, kind=ikind)
    xhat_d = nc.dram_tensor("xhat_d", [TOWN, D], BF16, kind=ikind)
    hhat_d = nc.dram_tensor("hhat_d", [TOWN, DFF], BF16, kind=ikind)
    gpart_d = nc.dram_tensor("gpart_d", [1, 1024], F32, kind=ikind)
    if DEBUG:
        dbg_x1 = nc.dram_tensor("dbg_x1", [128, ND * 1024], BF16, kind="ExternalOutput")
        dbg_k = nc.dram_tensor("dbg_k", [128, H * T], BF16, kind="ExternalOutput")
        dbg_q = nc.dram_tensor("dbg_q", [128, H * TOWN], BF16, kind="ExternalOutput")
        dbg_st = nc.dram_tensor("dbg_st", [128, 64], F32, kind="ExternalOutput")
        dbg_attn = nc.dram_tensor("dbg_attn", [128, H * TOWN], BF16, kind="ExternalOutput")

    mgr = []
    arena_cm = nc.sbuf_tensor("arena", [128, ARENA], U8)
    arena = arena_cm.__enter__()
    mgr.append(arena_cm)
    arena_base = nc.lookup_mloc(arena).addr

    def place(name, shape, dtype, off):
        h = nc.virtual_sbuf_tensor(name, shape, dtype)
        nc._tensor(h.name, h.shape, h.dtype, type="SB")
        mloc = nc.lookup_mloc(h)
        mloc.allocated = True
        mloc.addr = arena_base + off
        return h

    ident = place("identb", [128, 128], BF16, 0)
    mska = place("mska", [128, 128], F32, 256)
    mskb = place("mskb", [128, 128], F32, 768)
    bq_sb = place("bq_sb", [128, ND], F32, 1280)
    bk_sb = place("bk_sb", [128, ND], F32, 1344)
    ones_f = place("ones_f", [1, 128], F32, 1408)
    ones_b = place("ones_b", [1, 128], BF16, 1920)
    ones_c = place("ones_c", [128, 1], F32, 2176)
    st = place("st", [128, 64], F32, 2208)
    gval = place("gval", [1, 8], F32, 2464)
    thr = place("thr", [128, 8], F32, 2496)
    sfc = place("sfc", [128, 8], F32, 2528)
    spj = place("spj", [128, 8], F32, 2560)
    expb = place("expb", [128, 2], F32, 2592)
    lsum = place("lsum", [128, 4], F32, 2624)
    gacc = place("gacc", [128, 256], F32, 2656)
    junk = place("junk", [128, 512], F32, 3712)
    wch_a = place("wch_a", [128, 512], BF16, 5760)
    wch_b = place("wch_b", [128, 512], BF16, 6784)
    xq_c = place("xq_c", [128, 512], F32, 7808)
    out_c = place("out_c", [128, 512], F32, 9856)
    uq = place("uq", [128, 1024], F32, 11904)
    xhat_t = place("xhat_t", [128, 2048], BF16, 16000)
    x1t = place("x1t", [128, 2048], BF16, 20096)
    bv_sb = place("bv_sb", [1, 2048], BF16, 24192)
    bo_sb = place("bo_sb", [1, 2048], BF16, 28288)
    xt = place("xt", [128, 2048], F32, 32384)
    p_sb = place("p_sb", [128, 2048], BF16, 40576)
    pt_sb = place("pt_sb", [128, 1024], BF16, 44672)
    vbc = place("vbc", [128, 512], BF16, 46720)
    P_END = 47744
    A1 = P_END
    A2 = A1 + 65536
    A3 = A2 + 32768
    TRS = A3 + 32768
    k_sb = place("k_sb", [128, H * T], BF16, A1)
    xnew_sb = place("xnew_sb", [128, NOWN * D], F32, A1)
    h_lo = place("h_lo", [128, 4 * DFF], BF16, A1)
    x1kv = place("x1kv", [128, ND * 1024], BF16, A2)
    q_sb = place("q_sb", [128, H * TOWN], BF16, A2)
    wsl_kv = place("wsl_kv", [128, ND * 512], BF16, A3)
    x1q = place("x1q", [128, ND * 512], BF16, A3)
    wsl_q = place("wsl_q", [128, ND * 512], BF16, A3 + 16384)
    attn_sb = place("attn_sb", [128, H * TOWN], BF16, A3)
    h_hi = place("h_hi", [128, 4 * DFF], BF16, A3)
    g_a = place("g_a", [128, 1024], F32, TRS)
    g_b = place("g_b", [128, 1024], F32, TRS + 4096)
    w2 = place("w2", [128, 2048], F32, TRS)
    tin_a = place("tin_a", [128, 1024], F32, TRS + 8192)
    tin_b = place("tin_b", [128, 1024], F32, TRS + 12288)
    tm1 = place("tm1", [128, 1024], BF16, TRS + 16384)
    tm0 = place("tm0", [128, 1024], BF16, TRS + 18432)
    to1 = place("to1", [128, 1024], BF16, TRS + 20480)
    to2 = place("to2", [128, 1024], BF16, TRS + 22528)
    vs_a = place("vs_a", [128, 2048], BF16, TRS + 24576)
    vs_b = place("vs_b", [128, 2048], BF16, TRS + 28672)
    assert TRS + 32768 <= ARENA
    assert A3 + 65536 <= ARENA

    def h_ap(t, c0, w):
        base = h_lo if t < 4 else h_hi
        tt = t % 4
        return base[:, tt * DFF + c0: tt * DFF + c0 + w]

    def hfm_ap(o, c0, w):
        base = h_lo if o < 32 else h_hi
        oo = o % 32
        return base[:, oo * 1024 + c0: oo * 1024 + c0 + w]

    def psum(name, shape, dt):
        cm = nc.psum_tensor(name, shape, dt)
        h = cm.__enter__()
        mgr.append(cm)
        return h

    s_ps = psum("s_ps", [128, 2048], F32)
    pt_ps = psum("pt_ps", [128, 1024], F32)
    av_ps = psum("av_ps", [128, 512], F32)
    sm_ps = psum("sm_ps", [128, 512], F32)

    def bank_ap(b, w=512):
        t = [s_ps, s_ps, s_ps, s_ps, pt_ps, pt_ps, av_ps, sm_ps][b]
        off = [0, 512, 1024, 1536, 0, 512, 0, 0][b]
        return t[:, off: off + w]

    sem_names = ["sq", "gq", "cq", "tv", "vv", "sv", "gv"]
    SEM = {}
    for s in sem_names:
        cm = nc.semaphore(s)
        SEM[s] = cm.__enter__()
        mgr.append(cm)
    C = {s: 0 for s in sem_names}
    OPS = {"sync": [], "gpsimd": [], "tensor": [], "vector": [], "scalar": []}
    bank_ev = {i: [] for i in range(8)}

    def E(eng, fn, wait=(), inc=None, n=1):
        OPS[eng].append((list(wait), fn, inc, n))
        if inc is not None:
            C[inc] += n
        return C[inc] if inc else None

    def VS():
        return [("vv", C["vv"])]

    def SS():
        return [("sv", C["sv"])]

    def EV(fn, wait=(), inc="vv"):
        return E("vector", fn, wait=list(wait) + [("vv", C["vv"])], inc=inc)

    def ES(fn, wait=(), inc="sv"):
        return E("scalar", fn, wait=list(wait) + [("sv", C["sv"])], inc=inc)

    def dma(eng, sem, dst, src, wait=()):
        E(eng, lambda e, d=dst, s=src: e.dma_start(out=d, in_=s), wait=wait, inc=sem, n=16)

    def dmat(dst, src, wait=()):
        E("sync", lambda e, d=dst, s=src: e.dma_start(out=d, in_=s, transpose=True),
          wait=wait, inc="sq", n=16)

    def bwait(*banks):
        w = []
        for b in banks:
            for p in bank_ev[b]:
                if p not in w:
                    w.append(p)
        return w

    def bset(*banks, ev):
        for b in banks:
            bank_ev[b] = list(ev)

    dma("sync", "sq", ident[:, :], ident_in[:, :])
    dma("sync", "sq", mska[:, :], maska[:, :])
    dma("sync", "sq", mskb[:, :], maskb[:, :])
    dma("sync", "sq", bq_sb[:, :], bq[:, :])
    dma("sync", "sq", bk_sb[:, :], bk[:, :])
    dma("gpsimd", "gq", bv_sb[:, :], bv[:, :])
    dma("gpsimd", "gq", bo_sb[:, :], bo[:, :])
    consts_sq = C["sq"]
    consts_gq = C["gq"]
    E("gpsimd", lambda e: e.memset(ones_f[:, :], 1.0))
    E("gpsimd", lambda e: e.memset(ones_b[:, :], 1.0))
    E("gpsimd", lambda e: e.memset(ones_c[:, :], 1.0), inc="gv")
    consts_gv = C["gv"]

    def gsrc(i):
        if i < 128:
            s = i
            return wfcT[(s // 8) * 128:(s // 8 + 1) * 128, (s % 8) * 1024:(s % 8 + 1) * 1024]
        s = i - 128
        return wprojT[(s // 2) * 128:(s // 2 + 1) * 128, (s % 2) * 1024:(s % 2 + 1) * 1024]

    gbuf = [g_a, g_b]
    gamma_use = []
    gamma_n = [0]

    def gamma_pump2(k):
        for _ in range(k):
            i = gamma_n[0]
            if i >= 256:
                return
            gamma_n[0] = i + 1
            buf = gbuf[i % 2]
            w = [("vv", gamma_use[i - 2])] if i >= 2 else []
            w.append(("cq", C["cq"]))
            dma("scalar", "cq", buf[:, :], gsrc(i), wait=w)
            ld = C["cq"]
            wabs = [("cq", ld)]
            if gamma_use:
                wabs.append(("vv", gamma_use[-1]))
            E("scalar", lambda e, b=buf, col=i: e.activation(
                junk[:, :], b[:, 0:512], ACT.Abs, accum_out=gacc[:, col:col + 1]),
              wait=wabs, inc="sv")
            E("scalar", lambda e, b=buf: e.activation(
                junk[:, :], b[:, 512:1024], ACT.Abs, accum_out=st[:, 38:39]),
              inc="sv")
            sv2 = C["sv"]
            EV(lambda e, col=i: e.tensor_tensor(
                gacc[:, col:col + 1], gacc[:, col:col + 1], st[:, 38:39], ALU.add),
               wait=[("sv", sv2)])
            gamma_use.append(C["vv"])

    gamma_pump = gamma_pump2

    xt_use = [[], ]
    x1t_tuse = [0]

    def emit_ln1_tile(src_ap, dst_sb, dst_col, extra=()):
        w = list(extra) + list(xt_use[0])
        dma("sync", "sq", xt[:, :], src_ap, wait=w)
        ld = C["sq"]
        E("vector", lambda e: e.reduce_sum(st[:, 0:1], xt[:, :], AXX),
          wait=[("sq", ld)], inc="vv")
        E("scalar", lambda e: e.activation(junk[:, :], xt[:, 0:512], ACT.Square,
                                           accum_out=st[:, 1:2]),
          wait=[("sq", ld)], inc="sv")
        for cc in range(1, 4):
            E("scalar", lambda e, cc=cc: e.activation(
                junk[:, :], xt[:, cc * 512:(cc + 1) * 512], ACT.Square,
                accum_out=st[:, 1 + cc:2 + cc]), inc="sv")
        sden, vden = C["sv"], C["vv"]
        EV(lambda e: e.reduce_sum(st[:, 5:6], st[:, 1:5], AXX), wait=[("sv", sden)])
        EV(lambda e: e.tensor_scalar(st[:, 2:3], st[:, 0:1], 1.0 / D, None, ALU.mult),
           wait=[("vv", vden)])
        EV(lambda e: e.tensor_scalar(st[:, 3:4], st[:, 5:6], 1.0 / D, None, ALU.mult))
        EV(lambda e: e.tensor_tensor(st[:, 4:5], st[:, 2:3], st[:, 2:3], ALU.mult))
        EV(lambda e: e.scalar_tensor_tensor(st[:, 5:6], st[:, 3:4], EPS_LN, st[:, 4:5],
                                            ALU.add, ALU.subtract))
        vstat = C["vv"]
        ES(lambda e: e.activation(st[:, 6:7], st[:, 5:6], ACT.Ln), wait=[("vv", vstat)])
        ES(lambda e: e.activation(st[:, 7:8], st[:, 6:7], ACT.Exp, scale=-0.5))
        srd = C["sv"]
        EV(lambda e: e.tensor_scalar(st[:, 8:9], st[:, 2:3], -1.0, None, ALU.mult),
           wait=[("sv", srd)])
        EV(lambda e: e.tensor_tensor(st[:, 9:10], st[:, 8:9], st[:, 7:8], ALU.mult))
        vmr = C["vv"]
        wap = [("vv", vmr)]
        if x1t_tuse[0]:
            wap.append(("tv", x1t_tuse[0]))
        EV(lambda e: e.tensor_scalar(x1t[:, :], xt[:, :], st[:, 7:8], st[:, 9:10],
                                     ALU.mult, ALU.add), wait=wap)
        vap = C["vv"]
        xt_use[0] = [("vv", vap), ("sv", sden)]
        for g in range(4):
            gh = g % 2
            bk_ = 4 + gh
            for j in range(4):
                di = g * 4 + j
                wt = []
                if j == 0:
                    wt = [("vv", vap)] + bwait(bk_)
                E("tensor", lambda e, di=di, gh=gh, j=j: e.matmul(
                    pt_ps[:, gh * 512 + j * 128: gh * 512 + j * 128 + 128],
                    x1t[:, di * 128:(di + 1) * 128], ident[:, :], start=True, stop=True),
                    wait=wt, inc="tv")
            tpd = C["tv"]
            x1t_tuse[0] = tpd
            for j in range(4):
                di = g * 4 + j
                E("vector", lambda e, di=di, gh=gh, j=j, col=dst_col, dst=dst_sb: e.tensor_copy(
                    dst[:, di * (dst.shape[1] // ND) + col: di * (dst.shape[1] // ND) + col + 128],
                    pt_ps[:, gh * 512 + j * 128: gh * 512 + j * 128 + 128]),
                    wait=[("tv", tpd)] if j == 0 else [], inc="vv")
            bset(bk_, ev=[("vv", C["vv"])])
        return C["vv"]

    wsl_tuse = [0]
    x1_tuse = [0]
    vwr = 0

    for half in range(2):
        for i in range(8):
            tile = half * 8 + i
            extra = []
            if half == 0 and i == 0:
                extra = [("sq", consts_sq), ("gq", consts_gq), ("gv", consts_gv)]
            if x1_tuse[0]:
                extra = extra + [("tv", x1_tuse[0])]
            emit_ln1_tile(xkv[tile * 128:(tile + 1) * 128, :], x1kv, i * 128, extra)
            gamma_pump(6)
        x1_ready = C["vv"]
        if DEBUG and half == 0:
            dma("sync", "sq", dbg_x1[:, :], x1kv[:, :], wait=[("vv", x1_ready)])
            dma("sync", "sq", dbg_st[:, :], st[:, :], wait=[("vv", x1_ready)])

        for jq in range(4):
            for dt_ in range(ND):
                wt = [("tv", wsl_tuse[0])] if (dt_ == 0 and wsl_tuse[0]) else []
                if half == 0 and jq == 0 and dt_ == 0:
                    wt = wt + [("gq", consts_gq)]
                dma("gpsimd", "gq", wsl_kv[:, dt_ * 512:(dt_ + 1) * 512],
                    wk[dt_ * 128:(dt_ + 1) * 128, jq * 512:(jq + 1) * 512], wait=wt)
            wk_ld = C["gq"]
            for jj in range(4):
                j = jq * 4 + jj
                for c in range(2):
                    for dt_ in range(ND):
                        wt = []
                        if dt_ == 0:
                            wt = [("gq", wk_ld), ("vv", x1_ready)] + bwait(7)
                        E("tensor", lambda e, jj=jj, c=c, dt_=dt_: e.matmul(
                            sm_ps[:, 0:512],
                            wsl_kv[:, dt_ * 512 + jj * 128: dt_ * 512 + jj * 128 + 128],
                            x1kv[:, dt_ * 1024 + c * 512: dt_ * 1024 + c * 512 + 512],
                            start=(dt_ == 0), stop=(dt_ == ND - 1)), wait=wt, inc="tv")
                    mmk = C["tv"]
                    wsl_tuse[0] = mmk
                    x1_tuse[0] = mmk
                    E("scalar", lambda e, j=j, c=c, half=half: e.activation(
                        k_sb[:, j * T + half * 1024 + c * 512: j * T + half * 1024 + c * 512 + 512],
                        sm_ps[:, 0:512], ACT.Identity, bias=bk_sb[:, j:j + 1]),
                        wait=[("tv", mmk)], inc="sv")
                    bset(7, ev=[("sv", C["sv"])])
            gamma_pump(2)

        for jc in range(4):
            for dt_ in range(ND):
                wt = [("tv", wsl_tuse[0])] if dt_ == 0 else []
                dma("gpsimd", "gq", wsl_kv[:, dt_ * 512:(dt_ + 1) * 512],
                    wv[dt_ * 128:(dt_ + 1) * 128, jc * 512:(jc + 1) * 512], wait=wt)
            wv_ld = C["gq"]
            for tt in range(8):
                tok = half * 8 + tt
                for dt_ in range(ND):
                    wt = []
                    if dt_ == 0:
                        wt = [("gq", wv_ld)] + bwait(tt)
                    E("tensor", lambda e, tt=tt, dt_=dt_: e.matmul(
                        bank_ap(tt),
                        x1kv[:, dt_ * 1024 + tt * 128: dt_ * 1024 + tt * 128 + 128],
                        wsl_kv[:, dt_ * 512:(dt_ + 1) * 512],
                        start=(dt_ == 0), stop=False), wait=wt, inc="tv")
                E("tensor", lambda e, jc=jc, tt=tt: e.matmul(
                    bank_ap(tt), ones_b[:, :], bv_sb[:, jc * 512:(jc + 1) * 512],
                    start=False, stop=True), inc="tv")
                mmv = C["tv"]
                wsl_tuse[0] = mmv
                x1_tuse[0] = mmv
                wt = [("tv", mmv)]
                if vwr:
                    wt.append(("sq", vwr))
                E("vector", lambda e, tt=tt: e.tensor_copy(vbc[:, :], bank_ap(tt)),
                  wait=wt, inc="vv")
                vev = C["vv"]
                bset(tt, ev=[("vv", vev)])
                for hh in range(4):
                    h = jc * 4 + hh
                    dma("sync", "sq",
                        v_d[h * T + tok * 128: h * T + (tok + 1) * 128, :],
                        vbc[:, hh * 128:(hh + 1) * 128], wait=[("vv", vev)] if hh == 0 else [])
                vwr = C["sq"]
            gamma_pump(2)
    vwr_all = vwr

    for half in range(2):
        for i in range(4):
            tile = half * 4 + i
            extra = [("tv", x1_tuse[0])] if x1_tuse[0] else []
            emit_ln1_tile(xq[tile * 128:(tile + 1) * 128, :], x1q, i * 128, extra)
            gamma_pump(6)
        x1q_ready = C["vv"]
        for jq in range(4):
            for dt_ in range(ND):
                wt = [("tv", wsl_tuse[0])] if dt_ == 0 else []
                dma("gpsimd", "gq", wsl_q[:, dt_ * 512:(dt_ + 1) * 512],
                    wq[dt_ * 128:(dt_ + 1) * 128, jq * 512:(jq + 1) * 512], wait=wt)
            wq_ld = C["gq"]
            for jj in range(4):
                j = jq * 4 + jj
                for dt_ in range(ND):
                    wt = []
                    if dt_ == 0:
                        wt = [("gq", wq_ld), ("vv", x1q_ready)] + bwait(7)
                    E("tensor", lambda e, jj=jj, dt_=dt_: e.matmul(
                        sm_ps[:, 0:512],
                        wsl_q[:, dt_ * 512 + jj * 128: dt_ * 512 + jj * 128 + 128],
                        x1q[:, dt_ * 512: dt_ * 512 + 512],
                        start=(dt_ == 0), stop=(dt_ == ND - 1)), wait=wt, inc="tv")
                mmq = C["tv"]
                wsl_tuse[0] = mmq
                x1_tuse[0] = mmq
                E("scalar", lambda e, j=j, half=half: e.activation(
                    q_sb[:, j * TOWN + half * 512: j * TOWN + half * 512 + 512],
                    sm_ps[:, 0:512], ACT.Identity, bias=bq_sb[:, j:j + 1]),
                    wait=[("tv", mmq)], inc="sv")
                bset(7, ev=[("sv", C["sv"])])
            gamma_pump(4)

    gamma_pump(256)
    for c in range(4):
        EV(lambda e, c=c: e.reduce_sum(st[:, 10 + c:11 + c], gacc[:, c * 32:(c + 1) * 32], AXX),
           wait=[("vv", gamma_use[127])] if c == 0 else [])
    for c in range(4):
        EV(lambda e, c=c: e.reduce_sum(st[:, 14 + c:15 + c],
                                       gacc[:, 128 + c * 32:128 + (c + 1) * 32], AXX),
           wait=[("vv", gamma_use[255])] if c == 0 else [])
    vg = C["vv"]
    gp128 = gpart_d[0:1, 0:1024].rearrange("o (p c) -> (o p) c", p=128)
    dma("sync", "sq", gp128, st[:, 10:18], wait=[("vv", vg), ("sq", C["sq"])])
    gp_wr = C["sq"]
    dma("sync", "sq", g_b[0:1, 0:1024], gpart_d[0:1, 0:1024], wait=[("sq", gp_wr)])
    kst_ld = C["sq"]
    kS = st[0:1, 16:24]
    kC = st[0:1, 24:32]
    kY = st[0:1, 40:48]
    kT = st[0:1, 48:56]
    EV(lambda e: e.tensor_copy(kS, g_b[0:1, 0:8]), wait=[("sq", kst_ld)])
    EV(lambda e: e.tensor_scalar(kC, kS, 0.0, None, ALU.mult))
    for i in range(1, 128):
        xi = g_b[0:1, i * 8:(i + 1) * 8]
        EV(lambda e, xi=xi: e.tensor_tensor(kY, xi, kC, ALU.subtract))
        EV(lambda e: e.tensor_tensor(kT, kS, kY, ALU.add))
        EV(lambda e: e.tensor_tensor(kC, kT, kS, ALU.subtract))
        EV(lambda e: e.tensor_tensor(kC, kC, kY, ALU.subtract))
        EV(lambda e: e.tensor_copy(kS, kT))
    EV(lambda e: e.tensor_scalar(kC, kC, -1.0, None, ALU.mult))
    fS = st[0:1, 56:57]
    fC = st[0:1, 57:58]
    fY = st[0:1, 58:59]
    fT = st[0:1, 59:60]
    for g in range(2):
        lanes = ([st[0:1, 16 + g * 4 + k:17 + g * 4 + k] for k in range(4)] +
                 [st[0:1, 24 + g * 4 + k:25 + g * 4 + k] for k in range(4)])
        EV(lambda e, l0=lanes[0]: e.tensor_copy(fS, l0))
        EV(lambda e: e.tensor_scalar(fC, fS, 0.0, None, ALU.mult))
        for x in lanes[1:]:
            EV(lambda e, x=x: e.tensor_tensor(fY, x, fC, ALU.subtract))
            EV(lambda e: e.tensor_tensor(fT, fS, fY, ALU.add))
            EV(lambda e: e.tensor_tensor(fC, fT, fS, ALU.subtract))
            EV(lambda e: e.tensor_tensor(fC, fC, fY, ALU.subtract))
            EV(lambda e: e.tensor_copy(fS, fT))
        EV(lambda e: e.tensor_tensor(fT, fS, fC, ALU.subtract))
        EV(lambda e, g=g: e.tensor_scalar(gval[:, g:g + 1], fT, 1.0 / NW, None, ALU.mult))
    EV(lambda e: e.tensor_scalar(st[0:1, 62:63], gval[:, 0:1], 0.0, 0.5,
                                 ALU.mult, ALU.add))
    EV(lambda e: e.tensor_scalar(st[0:1, 63:64], st[0:1, 62:63], 2.0 ** -27,
                                 None, ALU.add))
    EV(lambda e: e.tensor_tensor(st[0:1, 63:64], st[0:1, 63:64], st[0:1, 62:63],
                                 ALU.subtract))
    EV(lambda e: e.tensor_scalar(st[0:1, 63:64], st[0:1, 63:64], -(2.0 ** 27), 1.0,
                                 ALU.mult, ALU.add))
    for g in range(2):
        EV(lambda e, g=g: e.tensor_scalar(st[0:1, 60:61], gval[:, g:g + 1],
                                          EPS_BL, 0.5, ALU.add, ALU.mult))
        EV(lambda e, g=g: e.tensor_scalar(st[0:1, 61:62], gval[:, g:g + 1],
                                          EPS_BL, 2.0 ** -25, ALU.add, ALU.mult))
        EV(lambda e: e.tensor_tensor(st[0:1, 61:62], st[0:1, 61:62], st[0:1, 63:64],
                                     ALU.mult))
        EV(lambda e, g=g: e.tensor_tensor(gval[:, 2 + g:3 + g], st[0:1, 60:61],
                                          st[0:1, 61:62], ALU.add))
        EV(lambda e, g=g: e.tensor_tensor(st[0:1, 62:63], gval[:, 2 + g:3 + g],
                                          st[0:1, 60:61], ALU.subtract))
        EV(lambda e, g=g: e.tensor_tensor(gval[:, 4 + g:5 + g], st[0:1, 61:62],
                                          st[0:1, 62:63], ALU.subtract))
        EV(lambda e, g=g: e.tensor_scalar(gval[:, 6 + g:7 + g], gval[:, 4 + g:5 + g],
                                          -1.0, None, ALU.mult))
    vgs = C["vv"]
    for i, sc in enumerate([2, 3, 0, 1, 4, 5, 6, 7]):
        E("tensor", lambda e, i=i, sc=sc: e.matmul(
            sm_ps[:, 2 + i:3 + i], ones_f[:, :], gval[:, sc:sc + 1], start=True, stop=True),
            wait=([("vv", vgs), ("gv", consts_gv)] + bwait(7) if i == 0 else []), inc="tv")
    tbc = C["tv"]
    E("vector", lambda e: e.tensor_copy(thr[:, 0:8], sm_ps[:, 2:10]), wait=[("tv", tbc)], inc="vv")
    thr_done = C["vv"]
    bset(7, ev=[("vv", thr_done)])

    tibuf = [tin_a, tin_b]
    tobuf = [to1, to2]
    tern_out = []
    tern_v = []
    tern_n = [0]

    def tern_pump(kk):
        for _ in range(kk):
            i = tern_n[0]
            if i >= 256:
                return
            tern_n[0] = i + 1
            if i < 128:
                s = i
                src = wfcT[(s // 8) * 128:(s // 8 + 1) * 128, (s % 8) * 1024:(s % 8 + 1) * 1024]
                dst = tfc_d[(s // 8) * 128:(s // 8 + 1) * 128, (s % 8) * 1024:(s % 8 + 1) * 1024]
                tc = 0
            else:
                s = i - 128
                src = wprojT[(s // 2) * 128:(s // 2 + 1) * 128, (s % 2) * 1024:(s % 2 + 1) * 1024]
                dst = tpj_d[(s // 2) * 128:(s // 2 + 1) * 128, (s % 2) * 1024:(s % 2 + 1) * 1024]
                tc = 1
            ib = tibuf[i % 2]
            ob = tobuf[i % 2]
            w = [("vv", tern_v[i - 2])] if i >= 2 else [("vv", thr_done)]
            w.append(("cq", C["cq"]))
            dma("scalar", "cq", ib[:, :], src, wait=w)
            ld = C["cq"]
            E("vector", lambda e, b=ib, tc=tc: e.tensor_scalar(
                g_a[:, :], b[:, :], thr[:, tc:tc + 1], None, ALU.subtract),
                wait=[("cq", ld), ("vv", thr_done)], inc="vv")
            E("vector", lambda e, tc=tc: e.tensor_scalar(
                tm1[:, :], g_a[:, :], thr[:, 4 + tc:5 + tc], None, ALU.is_gt), inc="vv")
            E("vector", lambda e, b=ib, tc=tc: e.tensor_scalar(
                g_a[:, :], b[:, :], thr[:, tc:tc + 1], None, ALU.add), inc="vv")
            E("vector", lambda e, tc=tc: e.tensor_scalar(
                tm0[:, :], g_a[:, :], thr[:, 6 + tc:7 + tc], None, ALU.is_lt), inc="vv")
            wst = [("gq", tern_out[i - 2])] if i >= 2 else []
            E("vector", lambda e, ob=ob: e.tensor_tensor(
                ob[:, :], tm1[:, :], tm0[:, :], ALU.subtract), wait=wst, inc="vv")
            tern_v.append(C["vv"])
            dma("gpsimd", "gq", dst, ob[:, :],
                wait=[("vv", tern_v[-1]), ("gq", C["gq"])])
            tern_out.append(C["gq"])

    prev_exp = [("sv", C["sv"])]
    prev_pt = [0]
    prev_avev = [0]
    vstr_use = [0, 0]
    vld_prev = [0, 0]
    pts_use = [0, 0]

    for u in range(8):
        width = (2 * u + 2) * 128
        nch = (width + 511) // 512
        nkt = 2 * u + 2
        for h in range(H):
            idx = u * H + h
            vb = [vs_a, vs_b][idx % 2]
            wt = [("sq", vwr_all), ("sq", kst_ld)] if idx == 0 else []
            if vstr_use[idx % 2]:
                wt.append(("tv", vstr_use[idx % 2]))
            dma("sync", "sq",
                vb[:, 0:width].rearrange("p (n c) -> p n c", c=128),
                v_d[h * T: h * T + width, :].rearrange("(n p) c -> p n c", p=128),
                wait=wt)
            vld = C["sq"]
            vld_prev[idx % 2] = vld

            for c in range(nch):
                w0 = c * 512
                cw = min(512, width - w0)
                wt = bwait(c) + ([("vv", thr_done)] if idx == 0 else [])
                E("tensor", lambda e, u=u, h=h, w0=w0, cw=cw: e.matmul(
                    s_ps[:, w0:w0 + cw],
                    q_sb[:, h * TOWN + u * 128: h * TOWN + u * 128 + 128],
                    k_sb[:, h * T + w0: h * T + w0 + cw],
                    start=True, stop=True), wait=wt, inc="tv")
            smm = C["tv"]
            E("vector", lambda e, u=u: e.tensor_tensor(
                s_ps[:, 2 * u * 128: 2 * u * 128 + 128],
                s_ps[:, 2 * u * 128: 2 * u * 128 + 128], mska[:, :], ALU.add),
                wait=[("tv", smm)], inc="vv")
            E("vector", lambda e, u=u: e.tensor_tensor(
                s_ps[:, (2 * u + 1) * 128: (2 * u + 1) * 128 + 128],
                s_ps[:, (2 * u + 1) * 128: (2 * u + 1) * 128 + 128], mskb[:, :], ALU.add),
                inc="vv")
            E("vector", lambda e, width=width: e.reduce_max(
                expb[:, 0:1], s_ps[:, 0:width], AXX, negate=True), inc="vv")
            EV(lambda e: e.tensor_scalar(expb[:, 1:2], expb[:, 0:1], SM_SCALE, None, ALU.mult))
            vmax = C["vv"]
            for c in range(nch):
                w0 = c * 512
                cw = min(512, width - w0)
                wt = []
                if c == 0:
                    wt = [("vv", vmax)]
                    if prev_pt[0]:
                        wt.append(("tv", prev_pt[0]))
                E("scalar", lambda e, w0=w0, cw=cw, c=c: e.activation(
                    p_sb[:, w0:w0 + cw], s_ps[:, w0:w0 + cw], ACT.Exp,
                    bias=expb[:, 1:2], scale=SM_SCALE, accum_out=lsum[:, c:c + 1]),
                    wait=wt, inc="sv")
            sexp = C["sv"]
            bset(*range(nch), ev=[("sv", sexp), ("vv", vmax)])
            EV(lambda e, nch=nch: e.reduce_sum(st[:, 12:13], lsum[:, 0:nch], AXX),
               wait=[("sv", sexp)])
            EV(lambda e: e.reciprocal(st[:, 13:14], st[:, 12:13]))
            EV(lambda e, width=width: e.tensor_scalar(
                p_sb[:, 0:width], p_sb[:, 0:width], st[:, 13:14], None, ALU.mult))
            vnorm = C["vv"]
            for g in range((nkt + 3) // 4):
                gh = g % 2
                bk_ = 4 + gh
                blocks = list(range(g * 4, min(g * 4 + 4, nkt)))
                for bi, kt in enumerate(blocks):
                    wt = []
                    if bi == 0:
                        wt = [("vv", vnorm)] + bwait(bk_)
                    E("tensor", lambda e, kt=kt, gh=gh, bi=bi: e.matmul(
                        pt_ps[:, gh * 512 + bi * 128: gh * 512 + bi * 128 + 128],
                        p_sb[:, kt * 128:(kt + 1) * 128], ident[:, :],
                        start=True, stop=True), wait=wt, inc="tv")
                tptg = C["tv"]
                prev_pt[0] = tptg
                wt = [("tv", tptg)]
                if pts_use[gh]:
                    wt.append(("tv", pts_use[gh]))
                E("vector", lambda e, gh=gh, nb=len(blocks): e.tensor_copy(
                    pt_sb[:, gh * 512: gh * 512 + nb * 128],
                    pt_ps[:, gh * 512: gh * 512 + nb * 128]), wait=wt, inc="vv")
                bset(bk_, ev=[("vv", C["vv"])])
                vevg = C["vv"]
                for bi, kt in enumerate(blocks):
                    wt = []
                    if bi == 0:
                        wt = [("vv", vevg), ("sq", vld)]
                        if kt == 0:
                            wt = wt + bwait(6)
                    E("tensor", lambda e, kt=kt, gh=gh, bi=bi, vb=vb,
                        first=(kt == 0), last=(kt == nkt - 1): e.matmul(
                        av_ps[:, 0:128],
                        vb[:, kt * 128:(kt + 1) * 128],
                        pt_sb[:, gh * 512 + bi * 128: gh * 512 + bi * 128 + 128],
                        start=first, stop=last), wait=wt, inc="tv")
                pts_use[gh] = C["tv"]
            tav = C["tv"]
            vstr_use[idx % 2] = tav
            E("vector", lambda e, h=h, u=u: e.tensor_copy(
                attn_sb[:, h * TOWN + u * 128: h * TOWN + u * 128 + 128], av_ps[:, 0:128]),
                wait=[("tv", tav)], inc="vv")
            bset(6, ev=[("vv", C["vv"])])
            tern_pump(2)

    if DEBUG:
        dma("sync", "sq", dbg_k[:, :], k_sb[:, :], wait=[("sv", C["sv"])])
        dma("sync", "sq", dbg_q[:, :], q_sb[:, :], wait=[("sv", C["sv"])])
        dma("sync", "sq", dbg_attn[:, :], attn_sb[:, :], wait=[("vv", C["vv"])])
    tern_pump(256)
    tern_all_gq = C["gq"]
    attn_done_v = C["vv"]
    attn_done_s = C["sv"]
    attn_done_t = C["tv"]

    op_ev = [0] * 8
    wch_tuse = [0, 0]
    xq_vuse = [0]
    for ec in range(4):
        for f in range(ND):
            i = ec * ND + f
            buf = wch_a if i % 2 == 0 else wch_b
            wt = [("tv", wch_tuse[i % 2])] if wch_tuse[i % 2] else [("vv", attn_done_v)]
            dma("gpsimd", "gq", buf[:, :],
                wo[f * 128:(f + 1) * 128, ec * 512:(ec + 1) * 512], wait=wt)
            wo_ld = C["gq"]
            for t in range(8):
                wt = []
                if t == 0:
                    wt = [("gq", wo_ld)]
                if f == 0:
                    wt = wt + bwait(t) + [("vv", attn_done_v)]
                E("tensor", lambda e, f=f, t=t, buf=buf: e.matmul(
                    bank_ap(t),
                    attn_sb[:, f * TOWN + t * 128: f * TOWN + t * 128 + 128],
                    buf[:, :], start=(f == 0), stop=False), wait=wt, inc="tv")
            wch_tuse[i % 2] = C["tv"]
        for t in range(8):
            E("tensor", lambda e, ec=ec, t=t: e.matmul(
                bank_ap(t), ones_b[:, :], bo_sb[:, ec * 512:(ec + 1) * 512],
                start=False, stop=True), inc="tv")
            topt = C["tv"]
            wt = [("vv", xq_vuse[0])] if xq_vuse[0] else []
            dma("sync", "sq", xq_c[:, :],
                xq[t * 128:(t + 1) * 128, ec * 512:(ec + 1) * 512], wait=wt)
            xq_ld = C["sq"]
            E("vector", lambda e, t=t, ec=ec: e.tensor_tensor(
                xnew_sb[:, t * D + ec * 512: t * D + ec * 512 + 512],
                bank_ap(t), xq_c[:, :], ALU.add),
                wait=[("tv", topt), ("sq", xq_ld)], inc="vv")
            op_ev[t] = C["vv"]
            xq_vuse[0] = C["vv"]
            bset(t, ev=[("vv", C["vv"])])

    xnew_wr = [0] * 8
    xhat_dma = [0]
    prev_tv = [C["vv"]]
    for t in range(8):
        dma("sync", "sq", xnew_d[t * 128:(t + 1) * 128, :],
            xnew_sb[:, t * D: t * D + D], wait=[("vv", op_ev[t])])
        xnew_wr[t] = C["sq"]
        E("vector", lambda e, t=t: e.reduce_sum(st[:, 16:17], xnew_sb[:, t * D: t * D + D], AXX),
          wait=[("vv", op_ev[t])], inc="vv")
        for cc in range(4):
            E("scalar", lambda e, t=t, cc=cc: e.activation(
                junk[:, :], xnew_sb[:, t * D + cc * 512: t * D + (cc + 1) * 512], ACT.Square,
                accum_out=st[:, 1 + cc:2 + cc]),
                wait=[("vv", prev_tv[0])] if cc == 0 else [], inc="sv")
        s0 = C["sv"]
        v0 = C["vv"]
        EV(lambda e: e.reduce_sum(st[:, 17:18], st[:, 1:5], AXX), wait=[("sv", s0)])
        EV(lambda e: e.tensor_scalar(st[:, 18:19], st[:, 16:17], 1.0 / D, None, ALU.mult),
           wait=[("vv", v0)])
        EV(lambda e: e.tensor_scalar(st[:, 19:20], st[:, 17:18], 1.0 / D, None, ALU.mult))
        EV(lambda e: e.tensor_tensor(st[:, 20:21], st[:, 18:19], st[:, 18:19], ALU.mult))
        EV(lambda e: e.scalar_tensor_tensor(st[:, 21:22], st[:, 19:20], EPS_LN,
                                            st[:, 20:21], ALU.add, ALU.subtract))
        vz2 = C["vv"]
        ES(lambda e: e.activation(st[:, 22:23], st[:, 21:22], ACT.Ln), wait=[("vv", vz2)])
        ES(lambda e: e.activation(st[:, 23:24], st[:, 22:23], ACT.Exp, scale=-0.5))
        sr2 = C["sv"]
        EV(lambda e, t=t: e.tensor_scalar(w2[:, :], xnew_sb[:, t * D: t * D + D],
                                          st[:, 18:19], None, ALU.subtract), wait=[("vv", vz2)])
        vw2 = C["vv"]
        EV(lambda e: e.reduce_sum(st[:, 24:25], w2[:, :], AXX))
        for cc in range(4):
            E("scalar", lambda e, cc=cc: e.activation(
                junk[:, :], w2[:, cc * 512:(cc + 1) * 512], ACT.Square,
                accum_out=st[:, 1 + cc:2 + cc]),
                wait=[("vv", vw2), ("sv", sr2)] if cc == 0 else [], inc="sv")
        sw2 = C["sv"]
        EV(lambda e: e.reduce_sum(st[:, 25:26], st[:, 1:5], AXX), wait=[("sv", sw2)])
        EV(lambda e: e.tensor_scalar(st[:, 26:27], st[:, 24:25], 1.0 / D, None, ALU.mult))
        EV(lambda e: e.tensor_scalar(w2[:, :], w2[:, :], st[:, 26:27], None, ALU.subtract))
        EV(lambda e: e.reduce_max(st[:, 27:28], w2[:, :], AXX, apply_absolute_value=True))
        vA = C["vv"]
        EV(lambda e: e.tensor_scalar(st[:, 28:29], st[:, 25:26], 1.0 / D, None, ALU.mult))
        EV(lambda e: e.tensor_tensor(st[:, 29:30], st[:, 26:27], st[:, 26:27], ALU.mult))
        EV(lambda e: e.tensor_tensor(st[:, 30:31], st[:, 28:29], st[:, 29:30], ALU.subtract))
        EV(lambda e: e.tensor_tensor(st[:, 31:32], st[:, 23:24], st[:, 23:24], ALU.mult))
        EV(lambda e: e.tensor_tensor(st[:, 32:33], st[:, 30:31], st[:, 31:32], ALU.mult))
        EV(lambda e: e.tensor_scalar(st[:, 33:34], st[:, 32:33], EPS_BL, None, ALU.add))
        vz3 = C["vv"]
        ES(lambda e: e.activation(st[:, 34:35], st[:, 33:34], ACT.Ln), wait=[("vv", vz3)])
        ES(lambda e: e.activation(st[:, 35:36], st[:, 34:35], ACT.Exp, scale=-0.5))
        sr3 = C["sv"]
        EV(lambda e: e.tensor_tensor(st[:, 36:37], st[:, 23:24], st[:, 27:28], ALU.mult),
           wait=[("sv", sr3)])
        EV(lambda e: e.tensor_scalar(st[:, 37:38], st[:, 36:37], st[:, 35:36], 1.0 / QB,
                                     ALU.mult, ALU.mult))
        EV(lambda e, t=t: e.tensor_tensor(sfc[:, t:t + 1], st[:, 37:38], thr[:, 2:3], ALU.mult))
        EV(lambda e: e.reciprocal(st[:, 39:40], st[:, 27:28]))
        EV(lambda e: e.tensor_scalar(st[:, 39:40], st[:, 39:40], QB, None, ALU.mult))
        wt = [("sq", xhat_dma[0])] if xhat_dma[0] else []
        EV(lambda e: e.tensor_scalar(xhat_t[:, :], w2[:, :], st[:, 39:40], None, ALU.mult),
           wait=wt)
        EV(lambda e: e.tensor_scalar(xhat_t[:, :], xhat_t[:, :], CLIP_HI, CLIP_LO,
                                     ALU.min, ALU.max))
        vxh = C["vv"]
        prev_tv[0] = vxh
        dma("sync", "sq", xhat_d[t * 128:(t + 1) * 128, :], xhat_t[:, :], wait=[("vv", vxh)])
        xhat_dma[0] = C["sq"]

    xhat_wr = C["sq"]
    sfc_done = C["vv"]
    for dt_ in range(ND):
        wt = [("sq", xhat_wr), ("tv", attn_done_t)] if dt_ == 0 else []
        dmat(q_sb[:, dt_ * 1024:(dt_ + 1) * 1024],
             xhat_d[:, dt_ * 128:(dt_ + 1) * 128], wait=wt)
    xfm_ld = C["sq"]

    fc_ev = [0] * 8
    fc_wuse = [0, 0]
    for oc in range(16):
        for dt_ in range(ND):
            i = oc * ND + dt_
            buf = wch_a if i % 2 == 0 else wch_b
            wt = [("tv", fc_wuse[i % 2])] if fc_wuse[i % 2] else []
            if i < 2:
                wt = wt + [("sq", xfm_ld), ("gq", tern_all_gq)]
            dma("sync", "sq", buf[:, :],
                tfc_d[dt_ * 128:(dt_ + 1) * 128, oc * 512:(oc + 1) * 512], wait=wt)
            tld = C["sq"]
            for t in range(8):
                wt = []
                if t == 0:
                    wt = [("sq", tld)]
                if dt_ == 0:
                    wt = wt + (bwait(t) if fc_ev[t] else bwait(t) + [("vv", sfc_done)])
                E("tensor", lambda e, dt_=dt_, t=t, buf=buf: e.matmul(
                    bank_ap(t),
                    q_sb[:, dt_ * 1024 + t * 128: dt_ * 1024 + t * 128 + 128],
                    buf[:, :], start=(dt_ == 0), stop=(dt_ == ND - 1)), wait=wt, inc="tv")
            fc_wuse[i % 2] = C["tv"]
        tfc_mm = C["tv"]
        for t in range(8):
            E("scalar", lambda e, t=t, oc=oc: e.activation(
                h_ap(t, oc * 512, 512), bank_ap(t), ACT.Gelu, scale=sfc[:, t:t + 1],
                accum_out=gacc[:, t * 16 + oc: t * 16 + oc + 1]),
                wait=[("tv", tfc_mm)], inc="sv")
            fc_ev[t] = C["sv"]
            bset(t, ev=[("sv", C["sv"])])

    hhat_dma = [0]
    prev_svq = [C["sv"]]
    for t in range(8):
        EV(lambda e, t=t: e.reduce_sum(st[:, 16:17], gacc[:, t * 16:(t + 1) * 16], AXX),
           wait=[("sv", fc_ev[t])])
        EV(lambda e: e.tensor_scalar(st[:, 18:19], st[:, 16:17], 1.0 / DFF, None, ALU.mult))
        vm = C["vv"]
        for oc in range(16):
            wt = []
            if oc == 0:
                wt = [("sv", fc_ev[t]), ("vv", prev_svq[0])]
            E("scalar", lambda e, t=t, oc=oc: e.activation(
                junk[:, :], h_ap(t, oc * 512, 512), ACT.Square,
                accum_out=st[:, 40 + oc:41 + oc]), wait=wt, inc="sv")
        ssq = C["sv"]
        EV(lambda e: e.reduce_sum(st[:, 17:18], st[:, 40:56], AXX), wait=[("sv", ssq)])
        EV(lambda e: e.tensor_scalar(st[:, 19:20], st[:, 17:18], 1.0 / DFF, None, ALU.mult))
        EV(lambda e: e.tensor_tensor(st[:, 20:21], st[:, 18:19], st[:, 18:19], ALU.mult))
        EV(lambda e: e.tensor_tensor(st[:, 21:22], st[:, 19:20], st[:, 20:21], ALU.subtract))
        EV(lambda e: e.tensor_scalar(st[:, 21:22], st[:, 21:22], EPS_BL, None, ALU.add))
        vz = C["vv"]
        for oc in range(16):
            EV(lambda e, t=t, oc=oc: e.tensor_scalar(
                uq[:, 0:512], h_ap(t, oc * 512, 512), st[:, 18:19], None, ALU.subtract),
                wait=[("vv", vm)] if oc == 0 else [])
            EV(lambda e, oc=oc: e.reduce_max(st[:, 40 + oc:41 + oc], uq[:, 0:512], AXX,
                                             apply_absolute_value=True))
        EV(lambda e: e.reduce_max(st[:, 27:28], st[:, 40:56], AXX))
        vA = C["vv"]
        ES(lambda e: e.activation(st[:, 34:35], st[:, 21:22], ACT.Ln), wait=[("vv", vz)])
        ES(lambda e: e.activation(st[:, 35:36], st[:, 34:35], ACT.Exp, scale=-0.5))
        sr = C["sv"]
        prev_svq[0] = C["vv"]
        EV(lambda e: e.tensor_scalar(st[:, 37:38], st[:, 27:28], 1.0 / QB, st[:, 35:36],
                                     ALU.mult, ALU.mult), wait=[("sv", sr), ("vv", vA)])
        EV(lambda e, t=t: e.tensor_tensor(spj[:, t:t + 1], st[:, 37:38], thr[:, 3:4], ALU.mult))
        EV(lambda e: e.reciprocal(st[:, 39:40], st[:, 27:28]), wait=[("vv", vA)])
        EV(lambda e: e.tensor_scalar(st[:, 39:40], st[:, 39:40], QB, None, ALU.mult))
        hq = 0
        for oc in range(16):
            wt = []
            if oc == 0 and hhat_dma[0]:
                wt.append(("sq", hhat_dma[0]))
            EV(lambda e, t=t, oc=oc: e.tensor_scalar(
                uq[:, 0:512], h_ap(t, oc * 512, 512), st[:, 18:19], None, ALU.subtract),
                wait=wt)
            EV(lambda e: e.tensor_scalar(uq[:, 512:1024], uq[:, 0:512], st[:, 39:40],
                                         None, ALU.mult))
            EV(lambda e, t=t, oc=oc: e.tensor_scalar(
                h_ap(t, oc * 512, 512), uq[:, 512:1024], CLIP_HI, CLIP_LO, ALU.min, ALU.max))
            hq = C["vv"]
        dma("sync", "sq", hhat_d[t * 128:(t + 1) * 128, :], h_ap(t, 0, DFF), wait=[("vv", hq)])
        hhat_dma[0] = C["sq"]
    hhat_wr = C["sq"]
    spj_done = C["vv"]

    for o in range(NO):
        wt = [("sq", hhat_wr)] if o == 0 else []
        dmat(hfm_ap(o, 0, 1024), hhat_d[:, o * 128:(o + 1) * 128], wait=wt)
    hfm_ld = C["sq"]

    pj_ev = [0] * 8
    pj_wuse = [0, 0]
    xn_vuse = [0]
    out_duse = [0]
    for ec in range(4):
        for o in range(NO):
            i = ec * NO + o
            buf = wch_a if i % 2 == 0 else wch_b
            wt = [("tv", pj_wuse[i % 2])] if pj_wuse[i % 2] else [("sq", hfm_ld)]
            dma("sync", "sq", buf[:, :],
                tpj_d[o * 128:(o + 1) * 128, ec * 512:(ec + 1) * 512], wait=wt)
            tld = C["sq"]
            for t in range(8):
                wt = []
                if t == 0:
                    wt = [("sq", tld)]
                if o == 0:
                    wt = wt + bwait(t) + ([("sq", hfm_ld)] if ec == 0 else [])
                E("tensor", lambda e, o=o, t=t, buf=buf: e.matmul(
                    bank_ap(t), hfm_ap(o, t * 128, 128),
                    buf[:, :], start=(o == 0), stop=(o == NO - 1)), wait=wt, inc="tv")
            pj_wuse[i % 2] = C["tv"]
        tpj_mm = C["tv"]
        for t in range(8):
            wt = [("sq", xnew_wr[t])]
            if xn_vuse[0]:
                wt.append(("vv", xn_vuse[0]))
            dma("sync", "sq", xq_c[:, :],
                xnew_d[t * 128:(t + 1) * 128, ec * 512:(ec + 1) * 512], wait=wt)
            xld = C["sq"]
            wt = [("tv", tpj_mm), ("sq", xld)]
            if t == 0 and ec == 0:
                wt.append(("vv", spj_done))
            if out_duse[0]:
                wt.append(("sq", out_duse[0]))
            E("vector", lambda e, t=t: e.scalar_tensor_tensor(
                out_c[:, :], bank_ap(t), spj[:, t:t + 1], xq_c[:, :], ALU.mult, ALU.add),
                wait=wt, inc="vv")
            pj_ev[t] = C["vv"]
            xn_vuse[0] = C["vv"]
            bset(t, ev=[("vv", C["vv"])])
            dma("sync", "sq", out[t * 128:(t + 1) * 128, ec * 512:(ec + 1) * 512], out_c[:, :],
                wait=[("vv", pj_ev[t])])
            out_duse[0] = C["sq"]

    final_sq = C["sq"]

    with nc.Block() as blk:
        def mk(eng_name):
            def body(e):
                for waits, fn, incs, n in OPS[eng_name]:
                    for (s, v) in waits:
                        if v > 0:
                            e.wait_ge(SEM[s], v)
                    ins = fn(e)
                    if incs is not None:
                        ins.then_inc(SEM[incs], n)
                if eng_name == "sync":
                    e.wait_ge(SEM["sq"], final_sq)
            return body

        blk.sync(mk("sync"))
        blk.gpsimd(mk("gpsimd"))
        blk.tensor(mk("tensor"))
        blk.vector(mk("vector"))
        blk.scalar(mk("scalar"))

    for cm in reversed(mgr):
        cm.__exit__(None, None, None)
    return nc


_CACHE = {}


def kernel(L0_attn_query, L0_attn_w_q, L0_attn_b_q, L0_attn_w_k, L0_attn_b_k,
           L0_attn_w_v, L0_attn_b_v, L0_attn_w_o, L0_attn_b_o, L0_ffn_w_fc, L0_ffn_w_proj):
    x = np.asarray(L0_attn_query, dtype=np.float32)
    wq_ = np.ascontiguousarray(np.asarray(L0_attn_w_q, np.float32))
    wk_ = np.ascontiguousarray(np.asarray(L0_attn_w_k, np.float32))
    wv_ = np.ascontiguousarray(np.asarray(L0_attn_w_v, np.float32))
    wo_ = np.ascontiguousarray(np.asarray(L0_attn_w_o, np.float32))
    bq_ = np.ascontiguousarray(np.asarray(L0_attn_b_q, np.float32).reshape(16, 128).T)
    bk_ = np.ascontiguousarray(np.asarray(L0_attn_b_k, np.float32).reshape(16, 128).T)
    bv_ = np.ascontiguousarray(np.asarray(L0_attn_b_v, np.float32).reshape(1, D))
    bo_ = np.ascontiguousarray(np.asarray(L0_attn_b_o, np.float32).reshape(1, D))
    wfcT_ = np.ascontiguousarray(np.asarray(L0_ffn_w_fc, np.float32).T)
    wprojT_ = np.ascontiguousarray(np.asarray(L0_ffn_w_proj, np.float32).T)
    ident_ = np.eye(128, dtype=ml_dtypes.bfloat16)

    tri = np.triu(np.full((128, 128), -1e10, np.float32), k=1)
    mask_sets = {
        0: (np.zeros((128, 128), np.float32), tri),
        1: (tri, np.full((128, 128), -1e10, np.float32)),
    }

    if "nc" not in _CACHE:
        _CACHE["nc"] = build_program()
    nc = _CACHE["nc"]

    in_maps = []
    row_sets = []
    for c in range(8):
        b, p = c // 2, c % 2
        qtiles = [2 * u + 1 - p for u in range(8)]
        rows = np.concatenate([np.arange(qt * 128, qt * 128 + 128) for qt in qtiles])
        row_sets.append((b, rows))
        ma, mb = mask_sets[p]
        in_maps.append({
            "xkv": np.ascontiguousarray(x[b]),
            "xq": np.ascontiguousarray(x[b][rows]),
            "wq": wq_, "wk": wk_, "wv": wv_, "wo": wo_,
            "bq": bq_, "bk": bk_, "bv": bv_, "bo": bo_,
            "wfcT": wfcT_, "wprojT": wprojT_,
            "maska": ma, "maskb": mb, "ident": ident_,
        })

    res = run_bass_kernel_spmd(nc, in_maps, core_ids=list(range(8)))
    outp = np.empty((B, T, D), np.float32)
    for c in range(8):
        b, rows = row_sets[c]
        outp[b][rows] = res.results[c]["out"]
    return outp
